# version 02
# baseline (speedup 1.0000x reference)
import numpy as np
import concourse.bass as bass
import concourse.mybir as mybir
from concourse.bass_utils import run_bass_kernel_spmd

F32 = mybir.dt.float32
BF16 = mybir.dt.bfloat16
AF = mybir.ActivationFunctionType
ALU = mybir.AluOpType
AX = mybir.AxisListType

D = 2048
DC = 16
T = 1024
THW = 512
TT = 8
NE = 7
NX = 8
NH = 32
NKV = 8
S = 2048
KC = 16
EPS = 1e-5
BIG = 1e30
ENGINES = ("sync", "gpsimd", "tensor", "scalar", "vector")


class Prog:
    def __init__(self, nc, sem_names):
        self.nc = nc
        self.sems = {n: nc.alloc_semaphore(n) for n in sem_names}
        self.cnt = {n: 0 for n in sem_names}
        self.items = {e: [] for e in ENGINES}
        self.last_wait = {e: {} for e in ENGINES}
        self.own = {"tensor": "pe", "scalar": "act", "vector": "dve", "gpsimd": "gp"}

    def wait(self, eng, ev):
        if ev is None:
            return
        s, v = ev
        if self.own.get(eng) == s:
            return
        if self.last_wait[eng].get(s, 0) >= v:
            return
        self.items[eng].append(("w", s, v))
        self.last_wait[eng][s] = v

    def op(self, eng, fn, waits=(), inc=None, inc_n=1):
        own = self.own.get(eng)
        if eng in ("scalar", "vector") and self.cnt.get(own, 0) > 0:
            if self.last_wait[eng].get(own, 0) < self.cnt[own]:
                self.items[eng].append(("w", own, self.cnt[own]))
                self.last_wait[eng][own] = self.cnt[own]
        for ev in waits:
            self.wait(eng, ev)
        incs = []
        ev_out = None
        if inc is not None:
            self.cnt[inc] += inc_n
            incs.append((inc, inc_n))
            ev_out = (inc, self.cnt[inc])
        self.items[eng].append(("o", fn, incs))
        return ev_out

    def emit(self):
        with self.nc.Block() as block:
            for eng_name in ENGINES:
                def body(eng, eng_name=eng_name):
                    for it in self.items[eng_name]:
                        if it[0] == "w":
                            eng.wait_ge(self.sems[it[1]], it[2])
                        else:
                            inst = it[1](eng)
                            for (s, n) in it[2]:
                                inst.then_inc(self.sems[s], n)
                getattr(block, eng_name)(body)


class Bank:

    def __init__(self, ap):
        self.ap = ap
        self.w_ev = None
        self.r_evs = []

    def pw(self):
        return [e for e in self.r_evs if e is not None]

    def wrote(self, ev):
        self.w_ev = ev
        self.r_evs = []

    def rw(self):
        return [self.w_ev] if self.w_ev else []

    def read(self, ev):
        if ev is not None:
            self.r_evs.append(ev)


def build(stop=None):
    nc = bass.Bass("TRN2", target_bir_lowering=False, debug=False)

    x_in = nc.dram_tensor("x", [TT, 128, D], F32, kind="ExternalInput").ap()
    o_out = nc.dram_tensor("o", [TT, 128, D], F32, kind="ExternalOutput").ap()
    w1g_in = nc.dram_tensor("w1g", [NX, DC, 128, 2 * D], F32, kind="ExternalInput").ap()
    w2_in = nc.dram_tensor("w2t", [NX, DC, 128, D], F32, kind="ExternalInput").ap()
    wq_in = nc.dram_tensor("wqt", [16, 128, D], F32, kind="ExternalInput").ap()
    wk_in = nc.dram_tensor("wkt", [4, 128, D], F32, kind="ExternalInput").ap()
    wv_in = nc.dram_tensor("wvt", [4, 128, D], F32, kind="ExternalInput").ap()
    wo_in = nc.dram_tensor("wot", [16, 128, D], F32, kind="ExternalInput").ap()
    gw_in = nc.dram_tensor("gwt", [128, DC * 7], F32, kind="ExternalInput").ap()
    br_in = nc.dram_tensor("br", [1, 7], F32, kind="ExternalInput").ap()
    gb_in = nc.dram_tensor("gb", [1, 7], F32, kind="ExternalInput").ap()

    bounce_in = nc.dram_tensor("bounce_in", [1024, T], BF16)
    bounce_out = nc.dram_tensor("bounce_out", [2048, T], BF16)
    ident_dram = nc.inline_tensor(np.eye(128, dtype=np.float32), name="idnt").ap()

    SBUF_BASE = 0x4080

    def A(name, shape, dtype, offset):
        return nc.alloc_sbuf_tensor_at(name, shape, dtype, offset=SBUF_BASE + offset)
    RES = A("RES", [128, DC * T], F32, offset=0x00000)
    XN = A("XN", [128, DC * T], BF16, offset=0x10000)
    HE = A("HE", [128, DC * T], BF16, offset=0x18000)
    W1G = [A(f"W1G{i}", [128, 2 * D], BF16, offset=0x20000 + i * 0x2000) for i in range(2)]
    W2S = [A(f"W2S{i}", [128, D], BF16, offset=0x24000 + i * 0x1000) for i in range(2)]
    TMX = [A(f"TMX{i}", [128, D], F32, offset=0x26000 + i * 0x2000) for i in range(2)]
    KVOWN = A("KVOWN", [128, 8 * T], BF16, offset=0x26000)
    PT = [A(f"PT{i}", [128, THW], BF16, offset=0x26000 + i * 0x400) for i in range(4)]
    KP = [A(f"KP{i}", [128, S], BF16, offset=0x27000 + i * 0x1000) for i in range(2)]
    QP = [A(f"QP{i}", [128, T], BF16, offset=0x29000 + i * 0x800) for i in range(2)]
    VS = A("VS", [128, S], BF16, offset=0x2A000)
    VT = [A(f"VT{i}", [128, KC * 65], BF16, offset=0x2B000 + i * 0x840) for i in range(2)]
    RB = A("RB", [128, THW], F32, offset=0x2C100)
    WB = [A(f"WB{i}", [128, T], F32, offset=0x2C900 + i * 0x1000) for i in range(2)]
    SIL = [A(f"SIL{i}", [128, THW], BF16, offset=0x2E900 + i * 0x400) for i in range(2)]
    T2 = [A(f"T2_{i}", [128, THW], BF16, offset=0x2F100 + i * 0x400) for i in range(2)]
    XSQ = [A(f"XSQ{i}", [128, T], BF16, offset=0x2F900 + i * 0x800) for i in range(2)]
    TMP = A("TMPF", [128, THW], F32, offset=0x30900)
    KSQ = A("KSQ", [128, S], BF16, offset=0x2E900)
    QSQ = A("QSQ", [128, T], BF16, offset=0x2F900)
    SROW = A("SROW", [128, T], F32, offset=0x31100)
    MB = [A(f"MB{i}", [128, THW], F32, offset=0x32100 + i * 0x800) for i in range(2)]
    GL = A("GL", [7, T], F32, offset=0x32100)
    MCOL = A("MCOL", [128, 8], F32, offset=0x33100)
    RCOL = A("RCOL", [128, 8], F32, offset=0x33120)
    AROW = A("AROW", [128, 8], F32, offset=0x33140)
    GT = A("GT", [128, TT * 7], F32, offset=0x33160)
    WGT = A("WGT", [128, TT * 7], F32, offset=0x33240)
    GM = [A(f"GM{i}", [128, 7], F32, offset=0x33320 + i * 0x20) for i in range(6)]
    RW = A("RW", [128, 16], F32, offset=0x333E0)
    IDF = A("IDF", [128, 128], F32, offset=0x33420)
    IDB = A("IDB", [128, 128], BF16, offset=0x33620)
    ONESF = A("ONESF", [128, 8], F32, offset=0x33720)
    ONESB = A("ONESB", [128, 8], BF16, offset=0x33740)
    ONESR = A("ONESR", [128, 128], F32, offset=0x33760)
    EPSC = A("EPSC", [128, 1], F32, offset=0x33960)
    GW = A("GW", [128, DC * 7], F32, offset=0x33980)
    GB = A("GB", [1, 7], F32, offset=0x33B60)
    GBBC = A("GBBC", [128, 7], F32, offset=0x33B80)
    BIASBC = A("BIASBC", [128, 7], F32, offset=0x33BA0)
    CSBC = A("CSBC", [128, 7], F32, offset=0x33BC0)
    BRROW = A("BRROW", [1, 7], F32, offset=0x33BE0)
    CSROW = A("CSROW", [1, 7], F32, offset=0x33C00)

    PS = [nc.alloc_psum_tensor(f"PS{i}", [128, 512], F32) for i in range(7)]
    PSB = nc.alloc_psum_tensor("PSB", [128, 1024], BF16)
    B = [Bank(PS[i].ap()) for i in range(7)]
    BB = Bank(PSB.ap())

    sem_names = (["pe", "act", "dve", "gp", "cst", "kvb", "cc", "odma0", "odma1", "vs", "dbg"]
                 + [f"w1g{i}" for i in range(2)] + [f"w2{i}" for i in range(2)]
                 + [f"x{i}" for i in range(2)]
                 + [f"kp{i}" for i in range(2)] + [f"qp{i}" for i in range(2)])
    p = Prog(nc, sem_names)

    ev_idf = p.op("sync", lambda e: e.dma_start(out=IDF[:, :], in_=ident_dram), inc="cst", inc_n=16)
    ev_br = p.op("sync", lambda e: e.dma_start(out=BRROW[:, :], in_=br_in), inc="cst", inc_n=16)
    ev_gb = p.op("sync", lambda e: e.dma_start(out=GB[:, :], in_=gb_in), inc="cst", inc_n=16)
    ev_gw = p.op("sync", lambda e: e.dma_start(out=GW[:, :], in_=gw_in), inc="cst", inc_n=16)
    ev_idf = ev_br = ev_gb = ev_gw = ("cst", p.cnt["cst"])

    p.op("gpsimd", lambda e: e.memset(ONESF[:, :], 1.0), inc="gp")
    p.op("gpsimd", lambda e: e.memset(ONESB[:, :], 1.0), inc="gp")
    p.op("gpsimd", lambda e: e.memset(ONESR[:, :], 1.0), inc="gp")
    p.op("gpsimd", lambda e: e.memset(EPSC[:, :], EPS), inc="gp")
    for i in range(2):
        p.op("gpsimd", lambda e, i=i: e.memset(
            VT[i].rearrange("p (k c) -> p k c", k=KC)[:, :, 64:65], 1.0), inc="gp")
    ev_ms = p.op("gpsimd", lambda e: e.memset(RW[:, :], 0.0), inc="gp")

    ev_idb = p.op("scalar", lambda e: e.copy(IDB[:, :], IDF[:, :]), waits=[ev_idf], inc="act")

    ev = p.op("tensor", lambda e: e.matmul(PS[6].ap()[:, 0:7], ONESR[0:1, 0:128],
                                           BRROW[0:1, 0:7], start=True, stop=True),
              waits=[ev_br, ev_ms] + B[6].pw(), inc="pe")
    B[6].wrote(ev)
    ev = p.op("scalar", lambda e: e.copy(BIASBC[:, :], PS[6].ap()[:, 0:7]), waits=B[6].rw(), inc="act")
    B[6].read(ev)
    ev = p.op("tensor", lambda e: e.matmul(PS[6].ap()[:, 0:7], ONESR[0:1, 0:128],
                                           GB[0:1, 0:7], start=True, stop=True),
              waits=[ev_gb] + B[6].pw(), inc="pe")
    B[6].wrote(ev)
    ev = p.op("scalar", lambda e: e.copy(GBBC[:, :], PS[6].ap()[:, 0:7]), waits=B[6].rw(), inc="act")
    B[6].read(ev)
    for kb in range(DC):
        ev = p.op("tensor", lambda e, kb=kb: e.matmul(
            PS[6].ap()[0:1, 0:7], ONESF[:, 0:1], GW[:, kb * 7:(kb + 1) * 7],
            start=(kb == 0), stop=(kb == DC - 1)),
            waits=([ev_gw] + B[6].pw()) if kb == 0 else (),
            inc="pe" if kb == DC - 1 else None)
    B[6].wrote(ev)
    ev_csrow = p.op("scalar", lambda e: e.copy(CSROW[:, :], PS[6].ap()[0:1, 0:7]),
                    waits=B[6].rw(), inc="act")
    B[6].read(ev_csrow)
    ev = p.op("tensor", lambda e: e.matmul(PS[6].ap()[:, 0:7], ONESR[0:1, 0:128],
                                           CSROW[0:1, 0:7], start=True, stop=True),
              waits=[ev_csrow] + B[6].pw(), inc="pe")
    B[6].wrote(ev)
    ev_consts = p.op("scalar", lambda e: e.copy(CSBC[:, :], PS[6].ap()[:, 0:7]),
                     waits=B[6].rw(), inc="act")
    B[6].read(ev_consts)

    def dump_and_finish(waits, what):
        srcs = {"R": RES, "X": XN, "H": HE, "K": KVOWN, "W": WGT}
        for w in what:
            srcb = srcs[w]
            dst = nc.dram_tensor(f"d{w}", [128, srcb.shape[1]], F32,
                                 kind="ExternalOutput").ap()
            p.op("gpsimd", lambda e, dst=dst, srcb=srcb: e.dma_start(
                out=dst, in_=srcb[:, :]), waits=waits, inc="dbg", inc_n=16)
        p.wait("gpsimd", ("dbg", p.cnt["dbg"]))
        p.emit()
        return nc

    x_free = [None, None]
    res_ready = None
    for t in range(TT):
        sl = t % 2
        ev_x = p.op("sync", lambda e, t=t, sl=sl: e.dma_start(out=TMX[sl][:, :], in_=x_in[t]),
                    waits=[x_free[sl]] if x_free[sl] else [], inc=f"x{sl}", inc_n=16)
        for cg in range(4):
            bk = B[cg % 2]
            evt = None
            for c4 in range(4):
                ci = cg * 4 + c4
                evt = p.op("tensor", lambda e, sl=sl, ci=ci, c4=c4, bk=bk: e.transpose(
                    bk.ap[:, c4 * 128:(c4 + 1) * 128],
                    TMX[sl][:, ci * 128:(ci + 1) * 128], IDF[:, :]),
                    waits=([ev_x, ev_idf] + bk.pw()) if c4 == 0 else (),
                    inc="pe" if c4 == 3 else None)
            bk.wrote(evt)
            if cg == 3:
                x_free[sl] = evt
            dst = RES.rearrange("p (c n) -> p c n", c=DC)[:, cg * 4:(cg + 1) * 4,
                                                          t * 128:(t + 1) * 128]
            eve = p.op("scalar", lambda e, bk=bk, dst=dst: e.copy(
                dst, bk.ap[:, 0:512].rearrange("p (c n) -> p c n", c=4)),
                waits=bk.rw(), inc="act")
            bk.read(eve)
            res_ready = eve

    def emit_ln(pre_evs, need_cols):
        sb = [B[2], B[3], B[4], B[5]]
        ci_end = [None] * DC
        ev_grp = [None] * 4
        for ci in range(DC):
            sl = ci % 2
            ev_c = p.op("vector", lambda e, ci=ci: e.tensor_copy(
                XN[:, ci * T:(ci + 1) * T], RES[:, ci * T:(ci + 1) * T]),
                waits=list(pre_evs) if ci == 0 else [], inc="dve")
            ev_s = p.op("vector", lambda e, ci=ci, sl=sl: e.tensor_tensor(
                XSQ[sl][:, :], RES[:, ci * T:(ci + 1) * T],
                RES[:, ci * T:(ci + 1) * T], ALU.mult),
                waits=[ci_end[ci - 2]] if ci >= 2 else [], inc="dve")
            for g in range(4):
                th = g % 2
                bk = sb[g]
                if g < 2:
                    rhs = XN[:, ci * T + th * THW: ci * T + th * THW + THW]
                    evd = ev_c
                else:
                    rhs = XSQ[sl][:, th * THW: th * THW + THW]
                    evd = ev_s
                wl = [evd] + (bk.pw() if ci == 0 else [])
                ev_grp[g] = p.op("tensor", lambda e, bk=bk, rhs=rhs, ci=ci: e.matmul(
                    bk.ap[0:1, 0:512], ONESB[:, 0:1], rhs,
                    start=(ci == 0), stop=(ci == DC - 1)),
                    waits=wl, inc="pe" if (g == 3 or ci == DC - 1) else None)
            ci_end[ci] = ev_grp[3]
        for g in range(4):
            sb[g].wrote(ev_grp[g])
        ev_apply = None
        ev_cols = None
        srow_free = []
        for th in range(2):
            mcols, scr = slice(0, THW), slice(THW, 2 * THW)
            ev_m = p.op("vector", lambda e, th=th: e.tensor_scalar(
                SROW[0:1, mcols], sb[th].ap[0:1, 0:512], 1.0 / D, None, ALU.mult),
                waits=sb[th].rw() + srow_free, inc="dve")
            srow_free = []
            sb[th].read(ev_m)
            p.op("vector", lambda e: e.tensor_tensor(
                SROW[0:1, scr], SROW[0:1, mcols], SROW[0:1, mcols], ALU.mult), inc="dve")
            ev_v = p.op("vector", lambda e, th=th: e.scalar_tensor_tensor(
                SROW[0:1, scr], sb[2 + th].ap[0:1, 0:512], 1.0 / D, SROW[0:1, scr],
                ALU.mult, ALU.subtract), waits=sb[2 + th].rw(), inc="dve")
            sb[2 + th].read(ev_v)
            p.op("scalar", lambda e: e.activation(
                SROW[0:1, scr], SROW[0:1, scr], AF.Ln, bias=EPSC[0:1, 0:1], scale=1.0),
                waits=[ev_v], inc="act")
            ev_r = p.op("scalar", lambda e: e.activation(
                SROW[0:1, scr], SROW[0:1, scr], AF.Exp, bias=0.0, scale=-0.5), inc="act")
            ev_mb = None
            for j, (cols, evrow) in enumerate([(mcols, ev_m), (scr, ev_r)]):
                bk = B[j]
                evb = p.op("tensor", lambda e, bk=bk, cols=cols: e.matmul(
                    bk.ap[:, 0:512], ONESR[0:1, 0:128], SROW[0:1, cols],
                    start=True, stop=True),
                    waits=[evrow] + bk.pw(), inc="pe")
                bk.wrote(evb)
                srow_free.append(evb)
                ev_mb = p.op("scalar", lambda e, bk=bk, j=j: e.copy(MB[j][:, :], bk.ap[:, 0:512]),
                             waits=bk.rw(), inc="act")
                bk.read(ev_mb)
            if need_cols:
                for j, (cols, evrow) in enumerate([(mcols, ev_m), (scr, ev_r)]):
                    bk = B[6]
                    evt = None
                    for q in range(4):
                        evt = p.op("tensor", lambda e, bk=bk, cols=cols, q=q: e.transpose(
                            bk.ap[:, q:q + 1],
                            SROW[0:1, cols.start + q * 128: cols.start + (q + 1) * 128],
                            IDF[0:1, 0:1]),
                            waits=([evrow] + bk.pw()) if q == 0 else (),
                            inc="pe" if q == 3 else None)
                    bk.wrote(evt)
                    srow_free.append(evt)
                    dst = (MCOL if j == 0 else RCOL)[:, th * 4:(th + 1) * 4]
                    ev_cols = p.op("scalar", lambda e, bk=bk, dst=dst: e.copy(dst, bk.ap[:, 0:4]),
                                   waits=bk.rw(), inc="act")
                    bk.read(ev_cols)
            for ci in range(DC):
                src = RES[:, ci * T + th * THW: ci * T + th * THW + THW]
                dst = XN[:, ci * T + th * THW: ci * T + th * THW + THW]
                p.op("vector", lambda e, src=src: e.tensor_tensor(
                    TMP[:, :], src, MB[0][:, :], ALU.subtract),
                    waits=[ev_mb, ev_grp[3]] if ci == 0 else [], inc="dve")
                ev_apply = p.op("vector", lambda e, dst=dst: e.tensor_tensor(
                    dst, TMP[:, :], MB[1][:, :], ALU.mult), inc="dve")
        return ev_apply, ev_cols

    w_free = {("w1g", 0): None, ("w1g", 1): None, ("w2", 0): None, ("w2", 1): None}

    def dma_w(kind, slot, src_ap, cols=None):
        tile = (W1G if kind == "w1g" else W2S)[slot]
        dst = tile[:, 0:cols] if cols else tile[:, :]
        fe = w_free[(kind, slot)]
        return p.op("gpsimd", lambda e, dst=dst, src_ap=src_ap: e.dma_start(out=dst, in_=src_ap),
                    waits=[fe] if fe else [], inc=f"{kind}{slot}", inc_n=16)

    wb_free = [None, None]

    def emit_ffn(we, xn_ev, col_ev):
        ev_gl = []
        for th in range(2):
            for ci in range(DC):
                ev = p.op("tensor", lambda e, ci=ci, th=th: e.matmul(
                    PS[6].ap()[0:7, 0:512], GW[:, ci * 7:(ci + 1) * 7],
                    RES[:, ci * T + th * THW: ci * T + th * THW + THW],
                    start=(ci == 0), stop=(ci == DC - 1)),
                    waits=([ev_consts, xn_ev] + B[6].pw()) if ci == 0 else (),
                    inc="pe" if ci == DC - 1 else None)
            B[6].wrote(ev)
            ev = p.op("scalar", lambda e, th=th: e.copy(
                GL[0:7, th * THW:(th + 1) * THW], PS[6].ap()[0:7, 0:512]),
                waits=B[6].rw(), inc="act")
            B[6].read(ev)
            ev_gl.append(ev)
        ev_gt = []
        for half in range(2):
            bk = B[6]
            evt = None
            for q in range(4):
                tt_i = half * 4 + q
                evt = p.op("tensor", lambda e, bk=bk, tt_i=tt_i, q=q: e.transpose(
                    bk.ap[:, q * 7:(q + 1) * 7],
                    GL[0:7, tt_i * 128:(tt_i + 1) * 128], IDF[0:7, 0:7]),
                    waits=([ev_gl[half], ev_gl[0]] + bk.pw()) if q == 0 else (),
                    inc="pe" if q == 3 else None)
            bk.wrote(evt)
            dst = GT.rearrange("p (t c) -> p t c", c=7)[:, half * 4:(half + 1) * 4, :]
            ev = p.op("scalar", lambda e, bk=bk, dst=dst: e.copy(
                dst, bk.ap[:, 0:28].rearrange("p (t c) -> p t c", c=7)),
                waits=bk.rw(), inc="act")
            bk.read(ev)
            ev_gt.append(ev)
        ev_w_all = None
        for tt_i in range(TT):
            gtc = slice(tt_i * 7, (tt_i + 1) * 7)
            mc, rc = MCOL[:, tt_i:tt_i + 1], RCOL[:, tt_i:tt_i + 1]
            w0 = [ev_gt[tt_i // 4], col_ev] if tt_i % 4 == 0 else []
            p.op("vector", lambda e, mc=mc: e.tensor_scalar(
                GM[0][:, :], CSBC[:, :], mc, None, ALU.mult), waits=w0, inc="dve")
            p.op("vector", lambda e, gtc=gtc: e.tensor_tensor(
                GM[0][:, :], GT[:, gtc], GM[0][:, :], ALU.subtract), inc="dve")
            p.op("vector", lambda e, rc=rc: e.tensor_scalar(
                GM[0][:, :], GM[0][:, :], rc, None, ALU.mult), inc="dve")
            p.op("vector", lambda e: e.tensor_tensor(
                GM[0][:, :], GM[0][:, :], GBBC[:, :], ALU.add), inc="dve")
            p.op("vector", lambda e: e.reduce_max(RW[:, 0:1], GM[0][:, :], axis=AX.X), inc="dve")
            ev_d = p.op("vector", lambda e: e.tensor_scalar(
                RW[:, 1:2], RW[:, 0:1], -1.0, None, ALU.mult), inc="dve")
            ev_e = p.op("scalar", lambda e: e.activation(
                GM[1][:, :], GM[0][:, :], AF.Exp, bias=RW[:, 1:2], scale=1.0),
                waits=[ev_d], inc="act")
            p.op("vector", lambda e: e.reduce_sum(RW[:, 2:3], GM[1][:, :], axis=AX.X),
                 waits=[ev_e], inc="dve")
            p.op("vector", lambda e: e.reciprocal(RW[:, 3:4], RW[:, 2:3]), inc="dve")
            p.op("vector", lambda e: e.tensor_scalar(
                GM[2][:, :], GM[1][:, :], RW[:, 3:4], None, ALU.mult), inc="dve")
            p.op("vector", lambda e: e.tensor_tensor(
                GM[3][:, :], GM[2][:, :], BIASBC[:, :], ALU.add), inc="dve")
            p.op("vector", lambda e: e.reduce_max(RW[:, 4:5], GM[3][:, :], axis=AX.X), inc="dve")
            p.op("vector", lambda e: e.tensor_scalar(
                GM[4][:, :], GM[3][:, :], RW[:, 4:5], None, ALU.is_ge), inc="dve")
            p.op("vector", lambda e: e.scalar_tensor_tensor(
                GM[4][:, :], GM[4][:, :], -BIG, GM[3][:, :], ALU.mult, ALU.add), inc="dve")
            p.op("vector", lambda e: e.reduce_max(RW[:, 5:6], GM[4][:, :], axis=AX.X), inc="dve")
            p.op("vector", lambda e: e.tensor_scalar(
                GM[5][:, :], GM[4][:, :], RW[:, 5:6], None, ALU.is_ge), inc="dve")
            p.op("vector", lambda e: e.scalar_tensor_tensor(
                GM[5][:, :], GM[5][:, :], -BIG, GM[4][:, :], ALU.mult, ALU.add), inc="dve")
            p.op("vector", lambda e: e.reduce_max(RW[:, 6:7], GM[5][:, :], axis=AX.X), inc="dve")
            p.op("vector", lambda e: e.tensor_scalar(
                GM[4][:, :], GM[3][:, :], RW[:, 6:7], None, ALU.is_ge), inc="dve")
            p.op("vector", lambda e: e.tensor_tensor(
                GM[2][:, :], GM[2][:, :], GM[4][:, :], ALU.mult), inc="dve")
            p.op("vector", lambda e: e.reduce_sum(RW[:, 7:8], GM[2][:, :], axis=AX.X), inc="dve")
            p.op("vector", lambda e: e.reciprocal(RW[:, 8:9], RW[:, 7:8]), inc="dve")
            ev_w_all = p.op("vector", lambda e, gtc=gtc: e.tensor_scalar(
                WGT[:, gtc], GM[2][:, :], RW[:, 8:9], 0.5, ALU.mult, ALU.mult), inc="dve")

        hb = [B[0], B[1], B[2], B[3]]
        cb = [B[4], B[5]]
        sil_free = [None, None]
        ev_res = None
        for e_i in range(NX):
            routed = e_i < NE
            if routed:
                ev_srow = None
                for half in range(2):
                    bk = B[6]
                    evr = None
                    for q in range(4):
                        tt_i = half * 4 + q
                        evr = p.op("tensor", lambda e, tt_i=tt_i, q=q, bk=bk, e_i=e_i: e.matmul(
                            bk.ap[0:1, q * 128:(q + 1) * 128],
                            WGT[:, tt_i * 7 + e_i: tt_i * 7 + e_i + 1], IDF[:, :],
                            start=True, stop=True),
                            waits=([ev_w_all] + bk.pw()) if q == 0 else (),
                            inc="pe" if q == 3 else None)
                    bk.wrote(evr)
                    ev_srow = p.op("scalar", lambda e, half=half, bk=bk: e.copy(
                        SROW[0:1, half * 512:(half + 1) * 512], bk.ap[0:1, 0:512]),
                        waits=bk.rw(), inc="act")
                    bk.read(ev_srow)
                ev_wb = None
                for th in range(2):
                    bk = B[6]
                    evb = p.op("tensor", lambda e, th=th, bk=bk: e.matmul(
                        bk.ap[:, 0:512], ONESR[0:1, 0:128],
                        SROW[0:1, th * 512:(th + 1) * 512], start=True, stop=True),
                        waits=[ev_srow] + bk.pw(), inc="pe")
                    bk.wrote(evb)
                    ev_wb = p.op("scalar", lambda e, th=th, bk=bk, e_i=e_i: e.copy(
                        WB[e_i % 2][:, th * THW:(th + 1) * THW], bk.ap[:, 0:512]),
                        waits=bk.rw() + ([wb_free[e_i % 2]] if wb_free[e_i % 2] else []),
                        inc="act")
                    bk.read(ev_wb)
            ev_he = None
            for hc in range(DC):
                sl = (e_i * DC + hc) % 2
                ev_wd = dma_w("w1g", sl, w1g_in[we + e_i, hc])
                for th in range(2):
                    gidx = hc * 2 + th
                    bA = hb[(gidx * 2) % 4]
                    bBk = hb[(gidx * 2 + 1) % 4]
                    evA = evBk = None
                    for kc in range(DC):
                        wl = ([ev_wd] + bA.pw() + ([xn_ev] if (e_i == 0 and hc == 0) else [])) \
                            if kc == 0 else ()
                        evA = p.op("tensor", lambda e, sl=sl, kc=kc, th=th, bA=bA: e.matmul(
                            bA.ap[:, :], W1G[sl][:, kc * 128:(kc + 1) * 128],
                            XN[:, kc * T + th * THW: kc * T + th * THW + THW],
                            start=(kc == 0), stop=(kc == DC - 1)),
                            waits=wl, inc="pe" if kc == DC - 1 else None)
                    bA.wrote(evA)
                    for kc in range(DC):
                        wl = bBk.pw() if kc == 0 else ()
                        evBk = p.op("tensor", lambda e, sl=sl, kc=kc, th=th, bBk=bBk: e.matmul(
                            bBk.ap[:, :], W1G[sl][:, D + kc * 128: D + (kc + 1) * 128],
                            XN[:, kc * T + th * THW: kc * T + th * THW + THW],
                            start=(kc == 0), stop=(kc == DC - 1)),
                            waits=wl, inc="pe" if kc == DC - 1 else None)
                    bBk.wrote(evBk)
                    if th == 1:
                        w_free[("w1g", sl)] = evBk
                    buf = gidx % 2
                    evs = p.op("scalar", lambda e, bA=bA, buf=buf: e.activation(
                        SIL[buf][:, :], bA.ap[:, :], AF.Silu),
                        waits=bA.rw() + ([sil_free[buf]] if sil_free[buf] else []), inc="act")
                    bA.read(evs)
                    hslice = HE[:, hc * T + th * THW: hc * T + th * THW + THW]
                    if routed:
                        evt2 = p.op("vector", lambda e, bBk=bBk, buf=buf: e.tensor_tensor(
                            T2[buf][:, :], SIL[buf][:, :], bBk.ap[:, :], ALU.mult),
                            waits=[evs] + bBk.rw(), inc="dve")
                        bBk.read(evt2)
                        sil_free[buf] = evt2
                        ev_he = p.op("vector", lambda e, hslice=hslice, buf=buf, e_i=e_i, th=th:
                                     e.tensor_tensor(hslice, T2[buf][:, :],
                                                     WB[e_i % 2][:, th * THW:(th + 1) * THW],
                                                     ALU.mult),
                                     waits=[ev_wb], inc="dve")
                    else:
                        ev_he = p.op("vector", lambda e, hslice=hslice, bBk=bBk, buf=buf:
                                     e.tensor_tensor(hslice, SIL[buf][:, :], bBk.ap[:, :],
                                                     ALU.mult),
                                     waits=[evs] + bBk.rw(), inc="dve")
                        bBk.read(ev_he)
                        sil_free[buf] = ev_he
            if routed:
                wb_free[e_i % 2] = ev_he
            for dc in range(DC):
                sl = (e_i * DC + dc) % 2
                ev_wd = dma_w("w2", sl, w2_in[we + e_i, dc])
                for th in range(2):
                    gidx = dc * 2 + th
                    bC = cb[gidx % 2]
                    evC = None
                    for kc in range(DC):
                        wl = ([ev_wd, ev_he] + bC.pw()) if kc == 0 else ()
                        evC = p.op("tensor", lambda e, sl=sl, kc=kc, th=th, bC=bC: e.matmul(
                            bC.ap[:, :], W2S[sl][:, kc * 128:(kc + 1) * 128],
                            HE[:, kc * T + th * THW: kc * T + th * THW + THW],
                            start=(kc == 0), stop=(kc == DC - 1)),
                            waits=wl, inc="pe" if kc == DC - 1 else None)
                    bC.wrote(evC)
                    if th == 1:
                        w_free[("w2", sl)] = evC
                    rslice = RES[:, dc * T + th * THW: dc * T + th * THW + THW]
                    if routed:
                        ev_res = p.op("vector", lambda e, rslice=rslice, bC=bC: e.tensor_tensor(
                            rslice, rslice, bC.ap[:, :], ALU.add), waits=bC.rw(), inc="dve")
                    else:
                        ev_res = p.op("vector", lambda e, rslice=rslice, bC=bC:
                                      e.scalar_tensor_tensor(rslice, bC.ap[:, :], 0.5, rslice,
                                                             ALU.mult, ALU.add),
                                      waits=bC.rw(), inc="dve")
                    bC.read(ev_res)
        return ev_res

    def emit_proj(w_dram, ncb, out_fn, xn_ev):
        done = None
        for cbi in range(ncb):
            sl = cbi % 2
            ev_wd = dma_w("w1g", sl, w_dram[cbi], cols=D)
            for th in range(2):
                gidx = cbi * 2 + th
                bk = B[gidx % 4]
                evG = None
                for kc in range(DC):
                    wl = ([ev_wd, xn_ev] + bk.pw()) if kc == 0 else ()
                    evG = p.op("tensor", lambda e, sl=sl, kc=kc, th=th, bk=bk: e.matmul(
                        bk.ap[:, :], W1G[sl][:, kc * 128:(kc + 1) * 128],
                        XN[:, kc * T + th * THW: kc * T + th * THW + THW],
                        start=(kc == 0), stop=(kc == DC - 1)),
                        waits=wl, inc="pe" if kc == DC - 1 else None)
                bk.wrote(evG)
                if th == 1:
                    w_free[("w1g", sl)] = evG
                dst = out_fn(cbi, th)
                done = p.op("scalar", lambda e, dst=dst, bk=bk: e.copy(dst, bk.ap[:, :]),
                            waits=bk.rw(), inc="act")
                bk.read(done)
        return done

    if stop == "p1":
        return dump_and_finish([res_ready], "R")
    xn1, col1 = emit_ln([res_ready], need_cols=True)
    if stop == "ln1":
        return dump_and_finish([xn1, col1], "RX")
    ffn1_done = emit_ffn(0, xn1, col1)
    if stop == "ffn1":
        return dump_and_finish([ffn1_done], "RXHW")
    xn2, _ = emit_ln([ffn1_done], need_cols=False)
    if stop == "ln2":
        return dump_and_finish([xn2], "RX")
    q_done = emit_proj(wq_in, 16,
                       lambda cbi, th: HE[:, cbi * T + th * THW: cbi * T + th * THW + THW], xn2)
    k_done = emit_proj(wk_in, 4,
                       lambda cbi, th: KVOWN[:, cbi * T + th * THW: cbi * T + th * THW + THW], xn2)
    v_done = emit_proj(wv_in, 4,
                       lambda cbi, th: KVOWN[:, (4 + cbi) * T + th * THW: (4 + cbi) * T + th * THW + THW], xn2)
    if stop == "qkv":
        return dump_and_finish([q_done, k_done, v_done], "RXHK")
    ev_kvb = p.op("sync", lambda e: e.dma_start(
        out=bounce_in.ap().rearrange("(c p) t -> p c t", p=128),
        in_=KVOWN.rearrange("p (c t) -> p c t", c=8)),
        waits=[k_done, v_done], inc="kvb", inc_n=16)
    for i in range(2):
        p.op("gpsimd", lambda e, i=i: e.memset(KP[i][64:65, :], -1.0),
             waits=[ev_kvb], inc="gp")
    ev_kpm = ("gp", p.cnt["gp"])
    ev_cc = p.op("gpsimd", lambda e: e.collective_compute(
        "AllGather", ALU.bypass,
        replica_groups=[[0, 1], [2, 3], [4, 5], [6, 7]],
        ins=[bounce_in.ap()], outs=[bounce_out.ap()]),
        waits=[ev_kvb], inc="cc")

    kp_free = [[], []]
    vs_free = []
    qp_free = [[], []]
    vt_free = [None, None]
    srow64_free = None

    def load_kp(kvh):
        par = kvh % 2
        c, off = kvh // 2, 64 * (kvh % 2)
        ev = None
        for r in range(2):
            src = bounce_out.ap()[r * 1024 + c * 128 + off: r * 1024 + c * 128 + off + 64, :]
            ev = p.op("sync", lambda e, src=src, par=par, r=r: e.dma_start(
                out=KP[par][0:64, r * 1024:(r + 1) * 1024], in_=src),
                waits=[ev_cc, ev_kpm] + kp_free[par], inc=f"kp{par}", inc_n=16)
        return ev

    def load_vs(kvh):
        c, off = 4 + kvh // 2, 64 * (kvh % 2)
        ev = None
        for r in range(2):
            src = bounce_out.ap()[r * 1024 + c * 128 + off: r * 1024 + c * 128 + off + 64, :]
            ev = p.op("sync", lambda e, src=src, r=r: e.dma_start(
                out=VS[0:64, r * 1024:(r + 1) * 1024], in_=src),
                waits=[ev_cc] + vs_free, inc="vs", inc_n=16)
        return ev

    def load_qp(hh):
        buf = hh % 2
        chunk, p0 = hh // 2, 64 * (hh % 2)
        src = HE[p0:p0 + 64, chunk * T:(chunk + 1) * T]
        return p.op("sync", lambda e, src=src, buf=buf: e.dma_start(
            out=QP[buf][0:64, 0:T], in_=src),
            waits=[q_done, ev_kvb] + qp_free[buf], inc=f"qp{buf}", inc_n=16)

    kp_evs = {0: load_kp(0)}
    vs_evs = {0: load_vs(0)}
    qp_evs = {0: load_qp(0)}
    pt_free = [None] * 4
    o_done = None

    for kvh in range(NKV):
        par = kvh % 2
        ev_kpd = kp_evs[kvh]
        ev_vsd = vs_evs[kvh]
        evt = None
        for halfk in range(2):
            for k8 in range(8):
                kc = halfk * 8 + k8
                wl = ([ev_vsd, ev_idb] + BB.pw() + ([vt_free[par]] if (halfk == 0 and vt_free[par]) else [])) \
                    if k8 == 0 else ()
                evt = p.op("tensor", lambda e, kc=kc, k8=k8: e.transpose(
                    PSB.ap()[:, k8 * 64:(k8 + 1) * 64], VS[0:64, kc * 128:(kc + 1) * 128],
                    IDB[0:64, 0:64]),
                    waits=wl, inc="pe" if k8 == 7 else None)
            BB.wrote(evt)
            dst = VT[par].rearrange("p (k c) -> p k c", k=KC)[:, halfk * 8:(halfk + 1) * 8, 0:64]
            evv = p.op("scalar", lambda e, dst=dst: e.copy(
                dst, PSB.ap()[:, 0:512].rearrange("p (k c) -> p k c", c=64)),
                waits=BB.rw(), inc="act")
            BB.read(evv)
        if kvh + 1 < NKV:
            vs_free = [evt]
            vs_evs[kvh + 1] = load_vs(kvh + 1)
        ev_vt = evv
        ev_ksq = p.op("vector", lambda e, par=par: e.tensor_tensor(
            KSQ[0:64, :], KP[par][0:64, :], KP[par][0:64, :], ALU.mult),
            waits=[ev_kpd], inc="dve")
        ev_red = []
        for j in range(4):
            bk = B[6 if j % 2 == 0 else 3]
            evm = p.op("tensor", lambda e, j=j, bk=bk: e.matmul(
                bk.ap[64:65, 0:512], ONESB[0:64, 0:1], KSQ[0:64, j * 512:(j + 1) * 512],
                start=True, stop=True),
                waits=[ev_ksq] + bk.pw(), inc="pe")
            bk.wrote(evm)
            evr = p.op("vector", lambda e, j=j, bk=bk: e.reduce_max(
                AROW[64:65, j:j + 1], bk.ap[64:65, 0:512], axis=AX.X),
                waits=bk.rw(), inc="dve")
            bk.read(evr)
            ev_red.append(evr)
        p.op("vector", lambda e: e.tensor_tensor(
            AROW[64:65, 4:5], AROW[64:65, 0:1], AROW[64:65, 1:2], ALU.max), inc="dve")
        p.op("vector", lambda e: e.tensor_tensor(
            AROW[64:65, 5:6], AROW[64:65, 2:3], AROW[64:65, 3:4], ALU.max), inc="dve")
        ev_m2 = p.op("vector", lambda e: e.tensor_tensor(
            AROW[64:65, 4:5], AROW[64:65, 4:5], AROW[64:65, 5:6], ALU.max), inc="dve")

        for hq in range(4):
            hh = kvh * 4 + hq
            buf = hh % 2
            ev_qpd = qp_evs[hh]
            if hh + 1 < NH:
                qp_evs[hh + 1] = load_qp(hh + 1)
            ev_qsq = p.op("vector", lambda e, buf=buf: e.tensor_tensor(
                QSQ[0:64, :], QP[buf][0:64, :], QP[buf][0:64, :], ALU.mult),
                waits=[ev_qpd], inc="dve")
            evq = None
            for th in range(2):
                bk = B[6]
                evm = p.op("tensor", lambda e, th=th, bk=bk: e.matmul(
                    bk.ap[64:65, 0:512], ONESB[0:64, 0:1], QSQ[0:64, th * THW:(th + 1) * THW],
                    start=True, stop=True),
                    waits=[ev_qsq] + bk.pw(), inc="pe")
                bk.wrote(evm)
                evq = p.op("vector", lambda e, th=th, bk=bk: e.tensor_scalar(
                    SROW[64:65, th * THW:(th + 1) * THW], bk.ap[64:65, 0:512],
                    AROW[64:65, 4:5], 0.5, ALU.add, ALU.mult),
                    waits=bk.rw() + [ev_m2] + ([srow64_free] if srow64_free else []),
                    inc="dve")
                bk.read(evq)
            ev_b = p.op("scalar", lambda e, buf=buf: e.copy(
                QP[buf][64:65, 0:T], SROW[64:65, 0:T]), waits=[evq], inc="act")
            evsm = None
            for qt in range(2):
                av_evs = []
                hqt_i = (kvh * 4 + hq) * 2 + qt
                bO = B[3 + 2 * (hqt_i % 2)]
                sB = [B[0], B[1], B[2], B[4]]

                def s_mm(kc, first):
                    sbk = sB[kc % 4]
                    wl = ([ev_b, ev_kpd] if first else []) + sbk.pw()
                    ev = p.op("tensor", lambda e, kc=kc, par=par, buf=buf, qt=qt, sbk=sbk:
                              e.matmul(sbk.ap[:, :],
                                       KP[par][0:65, kc * 128:(kc + 1) * 128],
                                       QP[buf][0:65, qt * THW:(qt + 1) * THW],
                                       start=True, stop=True),
                              waits=wl, inc="pe")
                    sbk.wrote(ev)
                    return ev

                for kc in range(4):
                    evsm = s_mm(kc, kc == 0)
                for kc in range(KC):
                    sbk = sB[kc % 4]
                    pslot = kc % 4
                    wl2 = sbk.rw() + ([pt_free[pslot]] if pt_free[pslot] else [])
                    evex = p.op("scalar", lambda e, pslot=pslot, sbk=sbk: e.activation(
                        PT[pslot][:, :], sbk.ap[:, :], AF.Exp, bias=0.0, scale=0.125),
                        waits=wl2, inc="act")
                    sbk.read(evex)
                    wl3 = [evex, ev_vt] + (bO.pw() if kc == 0 else [])
                    evav = p.op("tensor", lambda e, kc=kc, par=par, pslot=pslot, bO=bO:
                                e.matmul(bO.ap[0:65, :],
                                         VT[par][:, kc * 65:(kc + 1) * 65], PT[pslot][:, :],
                                         start=(kc == 0), stop=(kc == KC - 1)),
                                waits=wl3, inc="pe")
                    av_evs.append(evav)
                    pt_free[pslot] = evav
                    if kc + 4 < KC:
                        evsm = s_mm(kc + 4, False)
                bO.wrote(av_evs[-1])
                evrc = p.op("vector", lambda e, bO=bO: e.reciprocal(
                    SROW[64:65, 0:512], bO.ap[64:65, 0:512]),
                    waits=bO.rw(), inc="dve")
                bO.read(evrc)
                bk6 = B[6]
                evbc = p.op("tensor", lambda e, bk6=bk6: e.matmul(
                    bk6.ap[0:64, 0:512], ONESR[64:65, 0:64], SROW[64:65, 0:512],
                    start=True, stop=True),
                    waits=[evrc] + bk6.pw(), inc="pe")
                bk6.wrote(evbc)
                srow64_free = evbc
                evrb = p.op("scalar", lambda e, bk6=bk6: e.copy(RB[0:64, :], bk6.ap[0:64, 0:512]),
                            waits=bk6.rw(), inc="act")
                bk6.read(evrb)
                chunk, p0 = hh // 2, 64 * (hh % 2)
                dst = XN[p0:p0 + 64, chunk * T + qt * THW: chunk * T + qt * THW + THW]
                o_done = p.op("vector", lambda e, dst=dst, bO=bO: e.tensor_tensor(
                    dst, bO.ap[0:64, 0:512], RB[0:64, :], ALU.mult),
                    waits=[evrb], inc="dve")
                bO.read(o_done)
            qp_free[buf] = [evsm]
            vt_free[par] = av_evs[-1]
        kp_free[par] = [evsm, ev_ksq]
        if kvh + 1 < NKV:
            kp_evs[kvh + 1] = load_kp(kvh + 1)

    if stop == "attn":
        return dump_and_finish([o_done], "RX")
    attn_res = None
    for dc in range(DC):
        sl = dc % 2
        ev_wd = dma_w("w1g", sl, wo_in[dc], cols=D)
        for th in range(2):
            gidx = dc * 2 + th
            bk = B[gidx % 4]
            evG = None
            for oc in range(DC):
                wl = ([ev_wd, o_done] + bk.pw()) if oc == 0 else ()
                evG = p.op("tensor", lambda e, sl=sl, oc=oc, th=th, bk=bk: e.matmul(
                    bk.ap[:, :], W1G[sl][:, oc * 128:(oc + 1) * 128],
                    XN[:, oc * T + th * THW: oc * T + th * THW + THW],
                    start=(oc == 0), stop=(oc == DC - 1)),
                    waits=wl, inc="pe" if oc == DC - 1 else None)
            bk.wrote(evG)
            if th == 1:
                w_free[("w1g", sl)] = evG
            rslice = RES[:, dc * T + th * THW: dc * T + th * THW + THW]
            attn_res = p.op("vector", lambda e, rslice=rslice, bk=bk: e.tensor_tensor(
                rslice, rslice, bk.ap[:, :], ALU.add), waits=bk.rw(), inc="dve")
            bk.read(attn_res)

    if stop == "wo":
        return dump_and_finish([attn_res], "RX")
    xn3, col3 = emit_ln([attn_res], need_cols=True)
    if stop == "ln3":
        return dump_and_finish([xn3, col3], "RX")
    ffn2_done = emit_ffn(0, xn3, col3)
    if stop == "ffn2":
        return dump_and_finish([ffn2_done], "RXW")

    out_free = [None, None]
    n_out = 0
    for t in range(TT):
        sl = t % 2
        eve = None
        for cg in range(4):
            bk = B[cg % 2]
            evt = None
            for c4 in range(4):
                ci = cg * 4 + c4
                wl = []
                if c4 == 0:
                    wl = ([ffn2_done] if (t == 0 and cg == 0) else []) + bk.pw() + \
                        ([out_free[sl]] if (cg == 0 and out_free[sl]) else [])
                evt = p.op("tensor", lambda e, ci=ci, c4=c4, t=t, bk=bk: e.transpose(
                    bk.ap[:, c4 * 128:(c4 + 1) * 128],
                    RES[:, ci * T + t * 128: ci * T + (t + 1) * 128], IDF[:, :]),
                    waits=wl, inc="pe" if c4 == 3 else None)
            bk.wrote(evt)
            eve = p.op("scalar", lambda e, bk=bk, sl=sl, cg=cg: e.copy(
                TMX[sl][:, cg * 512:(cg + 1) * 512], bk.ap[:, 0:512]),
                waits=bk.rw() + ([out_free[sl]] if out_free[sl] else []), inc="act")
            bk.read(eve)
        ev_st = p.op("sync", lambda e, t=t, sl=sl: e.dma_start(out=o_out[t], in_=TMX[sl][:, :]),
                     waits=[eve], inc=f"odma{sl}", inc_n=16)
        n_out += 16
        out_free[sl] = ev_st
    p.wait("sync", ("odma0", 16 * 4))
    p.wait("sync", ("odma1", 16 * 4))

    p.emit()
    return nc


def _lhsT_tiles(W):
    K, M = W.shape
    return np.ascontiguousarray(
        W.reshape(K // 128, 128, M // 128, 128).transpose(2, 1, 0, 3).reshape(M // 128, 128, K))


def kernel(L0_attn_x, L0_attn_Wq, L0_attn_Wk, L0_attn_Wv, L0_attn_Wo,
           L0_ffn_W1, L0_ffn_Wg, L0_ffn_W2, L0_ffn_Ws1, L0_ffn_Wsg, L0_ffn_Ws2,
           L0_ffn_gate_w, L0_ffn_gate_b, L0_ffn_biases):
    x = np.asarray(L0_attn_x, np.float32)
    W1 = np.asarray(L0_ffn_W1, np.float32)
    Wg = np.asarray(L0_ffn_Wg, np.float32)
    W2 = np.asarray(L0_ffn_W2, np.float32)

    w1g = np.empty((NX, DC, 128, 2 * D), np.float32)
    w2t = np.empty((NX, DC, 128, D), np.float32)
    for e in range(NE):
        w1g[e, :, :, 0:D] = _lhsT_tiles(W1[e])
        w1g[e, :, :, D:2 * D] = _lhsT_tiles(Wg[e])
        w2t[e] = _lhsT_tiles(W2[e])
    w1g[NE, :, :, 0:D] = _lhsT_tiles(np.asarray(L0_ffn_Ws1, np.float32))
    w1g[NE, :, :, D:2 * D] = _lhsT_tiles(np.asarray(L0_ffn_Wsg, np.float32))
    w2t[NE] = _lhsT_tiles(np.asarray(L0_ffn_Ws2, np.float32))
    wqt = _lhsT_tiles(np.asarray(L0_attn_Wq, np.float32))
    wkt = _lhsT_tiles(np.asarray(L0_attn_Wk, np.float32))
    wvt = _lhsT_tiles(np.asarray(L0_attn_Wv, np.float32))
    wot = _lhsT_tiles(np.asarray(L0_attn_Wo, np.float32))
    gw = np.asarray(L0_ffn_gate_w, np.float32)
    gwt = np.ascontiguousarray(gw.reshape(DC, 128, 7).transpose(1, 0, 2).reshape(128, DC * 7))
    brr = np.ascontiguousarray(np.asarray(L0_ffn_biases, np.float32).reshape(1, 7))
    gbr = np.ascontiguousarray(np.asarray(L0_ffn_gate_b, np.float32).reshape(1, 7))

    nc = build()
    shared = {"w1g": w1g, "w2t": w2t, "wqt": wqt, "wkt": wkt, "wvt": wvt, "wot": wot,
              "gwt": gwt, "br": brr, "gb": gbr}
    in_maps = []
    for c in range(8):
        b, h = divmod(c, 2)
        m = dict(shared)
        m["x"] = np.ascontiguousarray(x[b, h * T:(h + 1) * T, :].reshape(TT, 128, D))
        in_maps.append(m)
    res = run_bass_kernel_spmd(nc, in_maps, core_ids=list(range(8)))
    out = np.empty((4, 2048, D), np.float32)
    for c in range(8):
        b, h = divmod(c, 2)
        out[b, h * T:(h + 1) * T, :] = np.asarray(res.results[c]["o"]).reshape(T, D)
    return out


# version 04
# speedup vs baseline: 1.1526x; 1.1526x over previous
import numpy as np
import concourse.bass as bass
import concourse.mybir as mybir
from concourse.bass_utils import run_bass_kernel_spmd

F32 = mybir.dt.float32
BF16 = mybir.dt.bfloat16
AF = mybir.ActivationFunctionType
ALU = mybir.AluOpType
AX = mybir.AxisListType

D = 2048
DC = 16
T = 1024
THW = 512
TT = 8
NE = 7
NX = 8
NH = 32
NKV = 8
S = 2048
KC = 16
EPS = 1e-5
BIG = 1e30
ENGINES = ("sync", "gpsimd", "tensor", "scalar", "vector")


class Prog:
    def __init__(self, nc, sem_names):
        self.nc = nc
        self.sems = {n: nc.alloc_semaphore(n) for n in sem_names}
        self.cnt = {n: 0 for n in sem_names}
        self.items = {e: [] for e in ENGINES}
        self.last_wait = {e: {} for e in ENGINES}
        self.own = {"tensor": "pe", "scalar": "act", "vector": "dve", "gpsimd": "gp"}

    def wait(self, eng, ev):
        if ev is None:
            return
        s, v = ev
        if self.own.get(eng) == s:
            return
        if self.last_wait[eng].get(s, 0) >= v:
            return
        self.items[eng].append(("w", s, v))
        self.last_wait[eng][s] = v

    def op(self, eng, fn, waits=(), inc=None, inc_n=1):
        own = self.own.get(eng)
        if eng in ("scalar", "vector") and self.cnt.get(own, 0) > 0:
            if self.last_wait[eng].get(own, 0) < self.cnt[own]:
                self.items[eng].append(("w", own, self.cnt[own]))
                self.last_wait[eng][own] = self.cnt[own]
        for ev in waits:
            self.wait(eng, ev)
        incs = []
        ev_out = None
        if inc is not None:
            self.cnt[inc] += inc_n
            incs.append((inc, inc_n))
            ev_out = (inc, self.cnt[inc])
        self.items[eng].append(("o", fn, incs))
        return ev_out

    def emit(self):
        with self.nc.Block() as block:
            for eng_name in ENGINES:
                def body(eng, eng_name=eng_name):
                    for it in self.items[eng_name]:
                        if it[0] == "w":
                            eng.wait_ge(self.sems[it[1]], it[2])
                        else:
                            inst = it[1](eng)
                            for (s, n) in it[2]:
                                inst.then_inc(self.sems[s], n)
                getattr(block, eng_name)(body)


class Bank:

    def __init__(self, ap):
        self.ap = ap
        self.w_ev = None
        self.r_evs = []

    def pw(self):
        return [e for e in self.r_evs if e is not None]

    def wrote(self, ev):
        self.w_ev = ev
        self.r_evs = []

    def rw(self):
        return [self.w_ev] if self.w_ev else []

    def read(self, ev):
        if ev is not None:
            self.r_evs.append(ev)


def build(stop=None):
    nc = bass.Bass("TRN2", target_bir_lowering=False, debug=False)

    x_in = nc.dram_tensor("x", [TT, 128, D], F32, kind="ExternalInput").ap()
    o_out = nc.dram_tensor("o", [TT, 128, D], F32, kind="ExternalOutput").ap()
    w1g_in = nc.dram_tensor("w1g", [NX, DC, 128, 2 * D], F32, kind="ExternalInput").ap()
    w2_in = nc.dram_tensor("w2t", [NX, DC, 128, D], F32, kind="ExternalInput").ap()
    wq_in = nc.dram_tensor("wqt", [16, 128, D], F32, kind="ExternalInput").ap()
    wk_in = nc.dram_tensor("wkt", [4, 128, D], F32, kind="ExternalInput").ap()
    wv_in = nc.dram_tensor("wvt", [4, 128, D], F32, kind="ExternalInput").ap()
    wo_in = nc.dram_tensor("wot", [16, 128, D], F32, kind="ExternalInput").ap()
    gw_in = nc.dram_tensor("gwt", [128, DC * 7], F32, kind="ExternalInput").ap()
    br_in = nc.dram_tensor("br", [1, 7], F32, kind="ExternalInput").ap()
    gb_in = nc.dram_tensor("gb", [1, 7], F32, kind="ExternalInput").ap()

    bounce_in = nc.dram_tensor("bounce_in", [1024, T], BF16)
    bounce_out = nc.dram_tensor("bounce_out", [2048, T], BF16)
    ident_dram = nc.inline_tensor(np.eye(128, dtype=np.float32), name="idnt").ap()

    SBUF_BASE = 0x4080

    def A(name, shape, dtype, offset):
        return nc.alloc_sbuf_tensor_at(name, shape, dtype, offset=SBUF_BASE + offset)
    RES = A("RES", [128, DC * T], F32, offset=0x00000)
    XN = A("XN", [128, DC * T], BF16, offset=0x10000)
    HE = A("HE", [128, DC * T], BF16, offset=0x18000)
    W1G = [A(f"W1G{i}", [128, 2 * D], BF16, offset=0x20000 + i * 0x2000) for i in range(2)]
    W2S = [A(f"W2S{i}", [128, D], BF16, offset=0x24000 + i * 0x1000) for i in range(2)]
    TMX = [A(f"TMX{i}", [128, D], F32, offset=0x26000 + i * 0x2000) for i in range(2)]
    KVOWN = A("KVOWN", [128, 8 * T], BF16, offset=0x26000)
    PT = [A(f"PT{i}", [128, THW], BF16, offset=0x26000 + i * 0x400) for i in range(4)]
    KP = [A(f"KP{i}", [128, S], BF16, offset=0x27000 + i * 0x1000) for i in range(2)]
    QP = [A(f"QP{i}", [128, T], BF16, offset=0x29000 + i * 0x800) for i in range(2)]
    VS = A("VS", [128, S], BF16, offset=0x2A000)
    VT = [A(f"VT{i}", [128, KC * 65], BF16, offset=0x2B000 + i * 0x840) for i in range(2)]
    RB = A("RB", [128, THW], F32, offset=0x2C100)
    WB = [A(f"WB{i}", [128, T], F32, offset=0x2C900 + i * 0x1000) for i in range(2)]
    SIL = [A(f"SIL{i}", [128, THW], BF16, offset=0x2E900 + i * 0x400) for i in range(2)]
    T2 = [A(f"T2_{i}", [128, THW], BF16, offset=0x2F100 + i * 0x400) for i in range(2)]
    XSQ = [A(f"XSQ{i}", [128, T], BF16, offset=0x2F900 + i * 0x800) for i in range(2)]
    TMP = A("TMPF", [128, THW], F32, offset=0x30900)
    KSQ = A("KSQ", [128, S], BF16, offset=0x2E900)
    QSQ = A("QSQ", [128, T], BF16, offset=0x2F900)
    SROW = A("SROW", [128, T], F32, offset=0x31100)
    MB = [A(f"MB{i}", [128, THW], F32, offset=0x32100 + i * 0x800) for i in range(2)]
    GL = A("GL", [7, T], F32, offset=0x32100)
    MCOL = A("MCOL", [128, 8], F32, offset=0x33100)
    RCOL = A("RCOL", [128, 8], F32, offset=0x33120)
    AROW = A("AROW", [128, 8], F32, offset=0x33140)
    GT = A("GT", [128, TT * 7], F32, offset=0x33160)
    WGT = A("WGT", [128, TT * 7], F32, offset=0x33240)
    GM = [A(f"GM{i}", [128, 7], F32, offset=0x33320 + i * 0x20) for i in range(6)]
    RW = A("RW", [128, 16], F32, offset=0x333E0)
    IDF = A("IDF", [128, 128], F32, offset=0x33420)
    IDB = A("IDB", [128, 128], BF16, offset=0x33620)
    ONESF = A("ONESF", [128, 8], F32, offset=0x33720)
    ONESB = A("ONESB", [128, 8], BF16, offset=0x33740)
    ONESR = A("ONESR", [128, 128], F32, offset=0x33760)
    EPSC = A("EPSC", [128, 1], F32, offset=0x33960)
    GW = A("GW", [128, DC * 7], F32, offset=0x33980)
    GB = A("GB", [1, 7], F32, offset=0x33B60)
    GBBC = A("GBBC", [128, 7], F32, offset=0x33B80)
    BIASBC = A("BIASBC", [128, 7], F32, offset=0x33BA0)
    CSBC = A("CSBC", [128, 7], F32, offset=0x33BC0)
    BRROW = A("BRROW", [1, 7], F32, offset=0x33BE0)
    CSROW = A("CSROW", [1, 7], F32, offset=0x33C00)

    PS = [nc.alloc_psum_tensor(f"PS{i}", [128, 512], F32) for i in range(7)]
    PSB = nc.alloc_psum_tensor("PSB", [128, 1024], BF16)
    B = [Bank(PS[i].ap()) for i in range(7)]
    BB = Bank(PSB.ap())

    sem_names = (["pe", "act", "dve", "gp", "cst", "kvb", "cc", "odma0", "odma1", "vs", "dbg"]
                 + [f"w1g{i}" for i in range(2)] + [f"w2{i}" for i in range(2)]
                 + [f"x{i}" for i in range(2)]
                 + [f"kp{i}" for i in range(2)] + [f"qp{i}" for i in range(2)])
    p = Prog(nc, sem_names)

    ev_idf = p.op("sync", lambda e: e.dma_start(out=IDF[:, :], in_=ident_dram), inc="cst", inc_n=16)
    ev_br = p.op("sync", lambda e: e.dma_start(out=BRROW[:, :], in_=br_in), inc="cst", inc_n=16)
    ev_gb = p.op("sync", lambda e: e.dma_start(out=GB[:, :], in_=gb_in), inc="cst", inc_n=16)
    ev_gw = p.op("sync", lambda e: e.dma_start(out=GW[:, :], in_=gw_in), inc="cst", inc_n=16)
    ev_idf = ev_br = ev_gb = ev_gw = ("cst", p.cnt["cst"])

    p.op("gpsimd", lambda e: e.memset(ONESF[:, :], 1.0), inc="gp")
    p.op("gpsimd", lambda e: e.memset(ONESB[:, :], 1.0), inc="gp")
    p.op("gpsimd", lambda e: e.memset(ONESR[:, :], 1.0), inc="gp")
    p.op("gpsimd", lambda e: e.memset(EPSC[:, :], EPS), inc="gp")
    for i in range(2):
        p.op("gpsimd", lambda e, i=i: e.memset(
            VT[i].rearrange("p (k c) -> p k c", k=KC)[:, :, 64:65], 1.0), inc="gp")
    ev_ms = p.op("gpsimd", lambda e: e.memset(RW[:, :], 0.0), inc="gp")

    ev_idb = p.op("scalar", lambda e: e.copy(IDB[:, :], IDF[:, :]), waits=[ev_idf], inc="act")

    ev = p.op("tensor", lambda e: e.matmul(PS[6].ap()[:, 0:7], ONESR[0:1, 0:128],
                                           BRROW[0:1, 0:7], start=True, stop=True),
              waits=[ev_br, ev_ms] + B[6].pw(), inc="pe")
    B[6].wrote(ev)
    ev = p.op("scalar", lambda e: e.copy(BIASBC[:, :], PS[6].ap()[:, 0:7]), waits=B[6].rw(), inc="act")
    B[6].read(ev)
    ev = p.op("tensor", lambda e: e.matmul(PS[6].ap()[:, 0:7], ONESR[0:1, 0:128],
                                           GB[0:1, 0:7], start=True, stop=True),
              waits=[ev_gb] + B[6].pw(), inc="pe")
    B[6].wrote(ev)
    ev = p.op("scalar", lambda e: e.copy(GBBC[:, :], PS[6].ap()[:, 0:7]), waits=B[6].rw(), inc="act")
    B[6].read(ev)
    for kb in range(DC):
        ev = p.op("tensor", lambda e, kb=kb: e.matmul(
            PS[6].ap()[0:1, 0:7], ONESF[:, 0:1], GW[:, kb * 7:(kb + 1) * 7],
            start=(kb == 0), stop=(kb == DC - 1)),
            waits=([ev_gw] + B[6].pw()) if kb == 0 else (),
            inc="pe" if kb == DC - 1 else None)
    B[6].wrote(ev)
    ev_csrow = p.op("scalar", lambda e: e.copy(CSROW[:, :], PS[6].ap()[0:1, 0:7]),
                    waits=B[6].rw(), inc="act")
    B[6].read(ev_csrow)
    ev = p.op("tensor", lambda e: e.matmul(PS[6].ap()[:, 0:7], ONESR[0:1, 0:128],
                                           CSROW[0:1, 0:7], start=True, stop=True),
              waits=[ev_csrow] + B[6].pw(), inc="pe")
    B[6].wrote(ev)
    ev_consts = p.op("scalar", lambda e: e.copy(CSBC[:, :], PS[6].ap()[:, 0:7]),
                     waits=B[6].rw(), inc="act")
    B[6].read(ev_consts)

    def dump_and_finish(waits, what):
        srcs = {"R": RES, "X": XN, "H": HE, "K": KVOWN, "W": WGT}
        for w in what:
            srcb = srcs[w]
            dst = nc.dram_tensor(f"d{w}", [128, srcb.shape[1]], F32,
                                 kind="ExternalOutput").ap()
            p.op("gpsimd", lambda e, dst=dst, srcb=srcb: e.dma_start(
                out=dst, in_=srcb[:, :]), waits=waits, inc="dbg", inc_n=16)
        p.wait("gpsimd", ("dbg", p.cnt["dbg"]))
        p.emit()
        return nc

    x_free = [None, None]
    res_ready = None
    for t in range(TT):
        sl = t % 2
        ev_x = p.op("sync", lambda e, t=t, sl=sl: e.dma_start(out=TMX[sl][:, :], in_=x_in[t]),
                    waits=[x_free[sl]] if x_free[sl] else [], inc=f"x{sl}", inc_n=16)
        for cg in range(4):
            bk = B[cg % 2]
            evt = None
            for c4 in range(4):
                ci = cg * 4 + c4
                evt = p.op("tensor", lambda e, sl=sl, ci=ci, c4=c4, bk=bk: e.transpose(
                    bk.ap[:, c4 * 128:(c4 + 1) * 128],
                    TMX[sl][:, ci * 128:(ci + 1) * 128], IDF[:, :]),
                    waits=([ev_x, ev_idf] + bk.pw()) if c4 == 0 else (),
                    inc="pe" if c4 == 3 else None)
            bk.wrote(evt)
            if cg == 3:
                x_free[sl] = evt
            dst = RES.rearrange("p (c n) -> p c n", c=DC)[:, cg * 4:(cg + 1) * 4,
                                                          t * 128:(t + 1) * 128]
            eve = p.op("scalar", lambda e, bk=bk, dst=dst: e.copy(
                dst, bk.ap[:, 0:512].rearrange("p (c n) -> p c n", c=4)),
                waits=bk.rw(), inc="act")
            bk.read(eve)
            res_ready = eve

    def emit_ln(pre_evs, need_cols):
        sb = [B[2], B[3], B[4], B[5]]
        ci_end = [None] * DC
        ev_grp = [None] * 4
        for ci in range(DC):
            sl = ci % 2
            ev_c = p.op("vector", lambda e, ci=ci: e.tensor_copy(
                XN[:, ci * T:(ci + 1) * T], RES[:, ci * T:(ci + 1) * T]),
                waits=list(pre_evs) if ci == 0 else [], inc="dve")
            ev_s = p.op("vector", lambda e, ci=ci, sl=sl: e.tensor_tensor(
                XSQ[sl][:, :], RES[:, ci * T:(ci + 1) * T],
                RES[:, ci * T:(ci + 1) * T], ALU.mult),
                waits=[ci_end[ci - 2]] if ci >= 2 else [], inc="dve")
            for g in range(4):
                th = g % 2
                bk = sb[g]
                if g < 2:
                    rhs = XN[:, ci * T + th * THW: ci * T + th * THW + THW]
                    evd = ev_c
                else:
                    rhs = XSQ[sl][:, th * THW: th * THW + THW]
                    evd = ev_s
                wl = [evd] + (bk.pw() if ci == 0 else [])
                ev_grp[g] = p.op("tensor", lambda e, bk=bk, rhs=rhs, ci=ci: e.matmul(
                    bk.ap[0:1, 0:512], ONESB[:, 0:1], rhs,
                    start=(ci == 0), stop=(ci == DC - 1)),
                    waits=wl, inc="pe" if (g == 3 or ci == DC - 1) else None)
            ci_end[ci] = ev_grp[3]
        for g in range(4):
            sb[g].wrote(ev_grp[g])
        ev_apply = None
        ev_cols = None
        srow_free = []
        for th in range(2):
            mcols, scr = slice(0, THW), slice(THW, 2 * THW)
            ev_m = p.op("vector", lambda e, th=th: e.tensor_scalar(
                SROW[0:1, mcols], sb[th].ap[0:1, 0:512], 1.0 / D, None, ALU.mult),
                waits=sb[th].rw() + srow_free, inc="dve")
            srow_free = []
            sb[th].read(ev_m)
            p.op("vector", lambda e: e.tensor_tensor(
                SROW[0:1, scr], SROW[0:1, mcols], SROW[0:1, mcols], ALU.mult), inc="dve")
            ev_v = p.op("vector", lambda e, th=th: e.scalar_tensor_tensor(
                SROW[0:1, scr], sb[2 + th].ap[0:1, 0:512], 1.0 / D, SROW[0:1, scr],
                ALU.mult, ALU.subtract), waits=sb[2 + th].rw(), inc="dve")
            sb[2 + th].read(ev_v)
            p.op("scalar", lambda e: e.activation(
                SROW[0:1, scr], SROW[0:1, scr], AF.Ln, bias=EPSC[0:1, 0:1], scale=1.0),
                waits=[ev_v], inc="act")
            ev_r = p.op("scalar", lambda e: e.activation(
                SROW[0:1, scr], SROW[0:1, scr], AF.Exp, bias=0.0, scale=-0.5), inc="act")
            ev_mb = None
            for j, (cols, evrow) in enumerate([(mcols, ev_m), (scr, ev_r)]):
                bk = B[j]
                evb = p.op("tensor", lambda e, bk=bk, cols=cols: e.matmul(
                    bk.ap[:, 0:512], ONESR[0:1, 0:128], SROW[0:1, cols],
                    start=True, stop=True),
                    waits=[evrow] + bk.pw(), inc="pe")
                bk.wrote(evb)
                srow_free.append(evb)
                ev_mb = p.op("scalar", lambda e, bk=bk, j=j: e.copy(MB[j][:, :], bk.ap[:, 0:512]),
                             waits=bk.rw(), inc="act")
                bk.read(ev_mb)
            if need_cols:
                for j, (cols, evrow) in enumerate([(mcols, ev_m), (scr, ev_r)]):
                    bk = B[6]
                    evt = None
                    for q in range(4):
                        evt = p.op("tensor", lambda e, bk=bk, cols=cols, q=q: e.transpose(
                            bk.ap[:, q:q + 1],
                            SROW[0:1, cols.start + q * 128: cols.start + (q + 1) * 128],
                            IDF[0:1, 0:1]),
                            waits=([evrow] + bk.pw()) if q == 0 else (),
                            inc="pe" if q == 3 else None)
                    bk.wrote(evt)
                    srow_free.append(evt)
                    dst = (MCOL if j == 0 else RCOL)[:, th * 4:(th + 1) * 4]
                    ev_cols = p.op("scalar", lambda e, bk=bk, dst=dst: e.copy(dst, bk.ap[:, 0:4]),
                                   waits=bk.rw(), inc="act")
                    bk.read(ev_cols)
            for ci in range(DC):
                src = RES[:, ci * T + th * THW: ci * T + th * THW + THW]
                dst = XN[:, ci * T + th * THW: ci * T + th * THW + THW]
                p.op("vector", lambda e, src=src: e.tensor_tensor(
                    TMP[:, :], src, MB[0][:, :], ALU.subtract),
                    waits=[ev_mb, ev_grp[3]] if ci == 0 else [], inc="dve")
                ev_apply = p.op("vector", lambda e, dst=dst: e.tensor_tensor(
                    dst, TMP[:, :], MB[1][:, :], ALU.mult), inc="dve")
        return ev_apply, ev_cols

    w_free = {("w1g", 0): None, ("w1g", 1): None, ("w2", 0): None, ("w2", 1): None}

    def dma_w(kind, slot, src_ap, cols=None):
        tile = (W1G if kind == "w1g" else W2S)[slot]
        dst = tile[:, 0:cols] if cols else tile[:, :]
        fe = w_free[(kind, slot)]
        return p.op("gpsimd", lambda e, dst=dst, src_ap=src_ap: e.dma_start(out=dst, in_=src_ap),
                    waits=[fe] if fe else [], inc=f"{kind}{slot}", inc_n=16)

    wb_free = [None, None]

    def emit_ffn(we, xn_ev, col_ev):
        ev_gl = []
        for th in range(2):
            for ci in range(DC):
                ev = p.op("tensor", lambda e, ci=ci, th=th: e.matmul(
                    PS[6].ap()[0:7, 0:512], GW[:, ci * 7:(ci + 1) * 7],
                    RES[:, ci * T + th * THW: ci * T + th * THW + THW],
                    start=(ci == 0), stop=(ci == DC - 1)),
                    waits=([ev_consts, xn_ev] + B[6].pw()) if ci == 0 else (),
                    inc="pe" if ci == DC - 1 else None)
            B[6].wrote(ev)
            ev = p.op("scalar", lambda e, th=th: e.copy(
                GL[0:7, th * THW:(th + 1) * THW], PS[6].ap()[0:7, 0:512]),
                waits=B[6].rw(), inc="act")
            B[6].read(ev)
            ev_gl.append(ev)
        ev_gt = []
        for half in range(2):
            bk = B[6]
            evt = None
            for q in range(4):
                tt_i = half * 4 + q
                evt = p.op("tensor", lambda e, bk=bk, tt_i=tt_i, q=q: e.transpose(
                    bk.ap[:, q * 7:(q + 1) * 7],
                    GL[0:7, tt_i * 128:(tt_i + 1) * 128], IDF[0:7, 0:7]),
                    waits=([ev_gl[half], ev_gl[0]] + bk.pw()) if q == 0 else (),
                    inc="pe" if q == 3 else None)
            bk.wrote(evt)
            dst = GT.rearrange("p (t c) -> p t c", c=7)[:, half * 4:(half + 1) * 4, :]
            ev = p.op("scalar", lambda e, bk=bk, dst=dst: e.copy(
                dst, bk.ap[:, 0:28].rearrange("p (t c) -> p t c", c=7)),
                waits=bk.rw(), inc="act")
            bk.read(ev)
            ev_gt.append(ev)
        ev_w_all = None
        for tt_i in range(TT):
            gtc = slice(tt_i * 7, (tt_i + 1) * 7)
            mc, rc = MCOL[:, tt_i:tt_i + 1], RCOL[:, tt_i:tt_i + 1]
            w0 = [ev_gt[tt_i // 4], col_ev] if tt_i % 4 == 0 else []
            p.op("vector", lambda e, mc=mc: e.tensor_scalar(
                GM[0][:, :], CSBC[:, :], mc, None, ALU.mult), waits=w0, inc="dve")
            p.op("vector", lambda e, gtc=gtc: e.tensor_tensor(
                GM[0][:, :], GT[:, gtc], GM[0][:, :], ALU.subtract), inc="dve")
            p.op("vector", lambda e, rc=rc: e.tensor_scalar(
                GM[0][:, :], GM[0][:, :], rc, None, ALU.mult), inc="dve")
            p.op("vector", lambda e: e.tensor_tensor(
                GM[0][:, :], GM[0][:, :], GBBC[:, :], ALU.add), inc="dve")
            p.op("vector", lambda e: e.reduce_max(RW[:, 0:1], GM[0][:, :], axis=AX.X), inc="dve")
            ev_d = p.op("vector", lambda e: e.tensor_scalar(
                RW[:, 1:2], RW[:, 0:1], -1.0, None, ALU.mult), inc="dve")
            ev_e = p.op("scalar", lambda e: e.activation(
                GM[1][:, :], GM[0][:, :], AF.Exp, bias=RW[:, 1:2], scale=1.0),
                waits=[ev_d], inc="act")
            p.op("vector", lambda e: e.reduce_sum(RW[:, 2:3], GM[1][:, :], axis=AX.X),
                 waits=[ev_e], inc="dve")
            p.op("vector", lambda e: e.reciprocal(RW[:, 3:4], RW[:, 2:3]), inc="dve")
            p.op("vector", lambda e: e.tensor_scalar(
                GM[2][:, :], GM[1][:, :], RW[:, 3:4], None, ALU.mult), inc="dve")
            p.op("vector", lambda e: e.tensor_tensor(
                GM[3][:, :], GM[2][:, :], BIASBC[:, :], ALU.add), inc="dve")
            p.op("vector", lambda e: e.reduce_max(RW[:, 4:5], GM[3][:, :], axis=AX.X), inc="dve")
            p.op("vector", lambda e: e.tensor_scalar(
                GM[4][:, :], GM[3][:, :], RW[:, 4:5], None, ALU.is_ge), inc="dve")
            p.op("vector", lambda e: e.scalar_tensor_tensor(
                GM[4][:, :], GM[4][:, :], -BIG, GM[3][:, :], ALU.mult, ALU.add), inc="dve")
            p.op("vector", lambda e: e.reduce_max(RW[:, 5:6], GM[4][:, :], axis=AX.X), inc="dve")
            p.op("vector", lambda e: e.tensor_scalar(
                GM[5][:, :], GM[4][:, :], RW[:, 5:6], None, ALU.is_ge), inc="dve")
            p.op("vector", lambda e: e.scalar_tensor_tensor(
                GM[5][:, :], GM[5][:, :], -BIG, GM[4][:, :], ALU.mult, ALU.add), inc="dve")
            p.op("vector", lambda e: e.reduce_max(RW[:, 6:7], GM[5][:, :], axis=AX.X), inc="dve")
            p.op("vector", lambda e: e.tensor_scalar(
                GM[4][:, :], GM[3][:, :], RW[:, 6:7], None, ALU.is_ge), inc="dve")
            p.op("vector", lambda e: e.tensor_tensor(
                GM[2][:, :], GM[2][:, :], GM[4][:, :], ALU.mult), inc="dve")
            p.op("vector", lambda e: e.reduce_sum(RW[:, 7:8], GM[2][:, :], axis=AX.X), inc="dve")
            p.op("vector", lambda e: e.reciprocal(RW[:, 8:9], RW[:, 7:8]), inc="dve")
            ev_w_all = p.op("vector", lambda e, gtc=gtc: e.tensor_scalar(
                WGT[:, gtc], GM[2][:, :], RW[:, 8:9], 0.5, ALU.mult, ALU.mult), inc="dve")

        hb = [B[0], B[1], B[2], B[3]]
        cb = [B[4], B[5]]
        sil_free = [None, None]
        ev_res = None
        for e_i in range(NX):
            routed = e_i < NE
            if routed:
                ev_srow = None
                for half in range(2):
                    bk = B[6]
                    evr = None
                    for q in range(4):
                        tt_i = half * 4 + q
                        evr = p.op("tensor", lambda e, tt_i=tt_i, q=q, bk=bk, e_i=e_i: e.matmul(
                            bk.ap[0:1, q * 128:(q + 1) * 128],
                            WGT[:, tt_i * 7 + e_i: tt_i * 7 + e_i + 1], IDF[:, :],
                            start=True, stop=True),
                            waits=([ev_w_all] + bk.pw()) if q == 0 else (),
                            inc="pe" if q == 3 else None)
                    bk.wrote(evr)
                    ev_srow = p.op("scalar", lambda e, half=half, bk=bk: e.copy(
                        SROW[0:1, half * 512:(half + 1) * 512], bk.ap[0:1, 0:512]),
                        waits=bk.rw(), inc="act")
                    bk.read(ev_srow)
                ev_wb = None
                for th in range(2):
                    bk = B[6]
                    evb = p.op("tensor", lambda e, th=th, bk=bk: e.matmul(
                        bk.ap[:, 0:512], ONESR[0:1, 0:128],
                        SROW[0:1, th * 512:(th + 1) * 512], start=True, stop=True),
                        waits=[ev_srow] + bk.pw(), inc="pe")
                    bk.wrote(evb)
                    ev_wb = p.op("scalar", lambda e, th=th, bk=bk, e_i=e_i: e.copy(
                        WB[e_i % 2][:, th * THW:(th + 1) * THW], bk.ap[:, 0:512]),
                        waits=bk.rw() + ([wb_free[e_i % 2]] if wb_free[e_i % 2] else []),
                        inc="act")
                    bk.read(ev_wb)
            ev_he = None
            for hc in range(DC):
                sl = (e_i * DC + hc) % 2
                ev_wd = dma_w("w1g", sl, w1g_in[we + e_i, hc])
                for th in range(2):
                    gidx = hc * 2 + th
                    bA = hb[(gidx * 2) % 4]
                    bBk = hb[(gidx * 2 + 1) % 4]
                    evA = evBk = None
                    for kc in range(DC):
                        wl = ([ev_wd] + bA.pw() + ([xn_ev] if (e_i == 0 and hc == 0) else [])) \
                            if kc == 0 else ()
                        evA = p.op("tensor", lambda e, sl=sl, kc=kc, th=th, bA=bA: e.matmul(
                            bA.ap[:, :], W1G[sl][:, kc * 128:(kc + 1) * 128],
                            XN[:, kc * T + th * THW: kc * T + th * THW + THW],
                            start=(kc == 0), stop=(kc == DC - 1)),
                            waits=wl, inc="pe" if kc == DC - 1 else None)
                    bA.wrote(evA)
                    for kc in range(DC):
                        wl = bBk.pw() if kc == 0 else ()
                        evBk = p.op("tensor", lambda e, sl=sl, kc=kc, th=th, bBk=bBk: e.matmul(
                            bBk.ap[:, :], W1G[sl][:, D + kc * 128: D + (kc + 1) * 128],
                            XN[:, kc * T + th * THW: kc * T + th * THW + THW],
                            start=(kc == 0), stop=(kc == DC - 1)),
                            waits=wl, inc="pe" if kc == DC - 1 else None)
                    bBk.wrote(evBk)
                    if th == 1:
                        w_free[("w1g", sl)] = evBk
                    buf = gidx % 2
                    evs = p.op("scalar", lambda e, bA=bA, buf=buf: e.activation(
                        SIL[buf][:, :], bA.ap[:, :], AF.Silu),
                        waits=bA.rw() + ([sil_free[buf]] if sil_free[buf] else []), inc="act")
                    bA.read(evs)
                    hslice = HE[:, hc * T + th * THW: hc * T + th * THW + THW]
                    if routed:
                        evt2 = p.op("vector", lambda e, bBk=bBk, buf=buf: e.tensor_tensor(
                            T2[buf][:, :], SIL[buf][:, :], bBk.ap[:, :], ALU.mult),
                            waits=[evs] + bBk.rw(), inc="dve")
                        bBk.read(evt2)
                        sil_free[buf] = evt2
                        ev_he = p.op("vector", lambda e, hslice=hslice, buf=buf, e_i=e_i, th=th:
                                     e.tensor_tensor(hslice, T2[buf][:, :],
                                                     WB[e_i % 2][:, th * THW:(th + 1) * THW],
                                                     ALU.mult),
                                     waits=[ev_wb], inc="dve")
                    else:
                        ev_he = p.op("vector", lambda e, hslice=hslice, bBk=bBk, buf=buf:
                                     e.tensor_tensor(hslice, SIL[buf][:, :], bBk.ap[:, :],
                                                     ALU.mult),
                                     waits=[evs] + bBk.rw(), inc="dve")
                        bBk.read(ev_he)
                        sil_free[buf] = ev_he
            if routed:
                wb_free[e_i % 2] = ev_he
            for dc in range(DC):
                sl = (e_i * DC + dc) % 2
                ev_wd = dma_w("w2", sl, w2_in[we + e_i, dc])
                for th in range(2):
                    gidx = dc * 2 + th
                    bC = cb[gidx % 2]
                    evC = None
                    for kc in range(DC):
                        wl = ([ev_wd, ev_he] + bC.pw()) if kc == 0 else ()
                        evC = p.op("tensor", lambda e, sl=sl, kc=kc, th=th, bC=bC: e.matmul(
                            bC.ap[:, :], W2S[sl][:, kc * 128:(kc + 1) * 128],
                            HE[:, kc * T + th * THW: kc * T + th * THW + THW],
                            start=(kc == 0), stop=(kc == DC - 1)),
                            waits=wl, inc="pe" if kc == DC - 1 else None)
                    bC.wrote(evC)
                    if th == 1:
                        w_free[("w2", sl)] = evC
                    rslice = RES[:, dc * T + th * THW: dc * T + th * THW + THW]
                    if routed:
                        ev_res = p.op("vector", lambda e, rslice=rslice, bC=bC: e.tensor_tensor(
                            rslice, rslice, bC.ap[:, :], ALU.add), waits=bC.rw(), inc="dve")
                    else:
                        ev_res = p.op("vector", lambda e, rslice=rslice, bC=bC:
                                      e.scalar_tensor_tensor(rslice, bC.ap[:, :], 0.5, rslice,
                                                             ALU.mult, ALU.add),
                                      waits=bC.rw(), inc="dve")
                    bC.read(ev_res)
        return ev_res

    def emit_proj(w_dram, ncb, out_fn, xn_ev):
        done = None
        for cbi in range(ncb):
            sl = cbi % 2
            ev_wd = dma_w("w1g", sl, w_dram[cbi], cols=D)
            for th in range(2):
                gidx = cbi * 2 + th
                bk = B[gidx % 4]
                evG = None
                for kc in range(DC):
                    wl = ([ev_wd, xn_ev] + bk.pw()) if kc == 0 else ()
                    evG = p.op("tensor", lambda e, sl=sl, kc=kc, th=th, bk=bk: e.matmul(
                        bk.ap[:, :], W1G[sl][:, kc * 128:(kc + 1) * 128],
                        XN[:, kc * T + th * THW: kc * T + th * THW + THW],
                        start=(kc == 0), stop=(kc == DC - 1)),
                        waits=wl, inc="pe" if kc == DC - 1 else None)
                bk.wrote(evG)
                if th == 1:
                    w_free[("w1g", sl)] = evG
                dst = out_fn(cbi, th)
                done = p.op("scalar", lambda e, dst=dst, bk=bk: e.copy(dst, bk.ap[:, :]),
                            waits=bk.rw(), inc="act")
                bk.read(done)
        return done

    if stop == "p1":
        return dump_and_finish([res_ready], "R")
    xn1, col1 = emit_ln([res_ready], need_cols=True)
    if stop == "ln1":
        return dump_and_finish([xn1, col1], "RX")
    ffn1_done = emit_ffn(0, xn1, col1)
    if stop == "ffn1":
        return dump_and_finish([ffn1_done], "RXHW")
    xn2, _ = emit_ln([ffn1_done], need_cols=False)
    if stop == "ln2":
        return dump_and_finish([xn2], "RX")
    k_done = emit_proj(wk_in, 4,
                       lambda cbi, th: KVOWN[:, cbi * T + th * THW: cbi * T + th * THW + THW], xn2)
    v_done = emit_proj(wv_in, 4,
                       lambda cbi, th: KVOWN[:, (4 + cbi) * T + th * THW: (4 + cbi) * T + th * THW + THW], xn2)
    ev_kvb = p.op("sync", lambda e: e.dma_start(
        out=bounce_in.ap().rearrange("(c p) t -> p c t", p=128),
        in_=KVOWN.rearrange("p (c t) -> p c t", c=8)),
        waits=[k_done, v_done], inc="kvb", inc_n=16)
    q_done = emit_proj(wq_in, 16,
                       lambda cbi, th: HE[:, cbi * T + th * THW: cbi * T + th * THW + THW], xn2)
    if stop == "qkv":
        return dump_and_finish([q_done, k_done, v_done], "RXHK")
    for i in range(2):
        p.op("gpsimd", lambda e, i=i: e.memset(KP[i][64:65, :], -1.0),
             waits=[ev_kvb], inc="gp")
    ev_kpm = ("gp", p.cnt["gp"])
    ev_cc = p.op("gpsimd", lambda e: e.collective_compute(
        "AllGather", ALU.bypass,
        replica_groups=[[0, 1], [2, 3], [4, 5], [6, 7]],
        ins=[bounce_in.ap()], outs=[bounce_out.ap()]),
        waits=[ev_kvb], inc="cc")

    kp_free = [[], []]
    vs_free = []
    qp_free = [[], []]
    vt_free = [None, None]
    srow64_guard = []
    qsq_free = []
    b_read_guard = []

    def load_kp(kvh):
        par = kvh % 2
        c, off = kvh // 2, 64 * (kvh % 2)
        ev = None
        for r in range(2):
            src = bounce_out.ap()[r * 1024 + c * 128 + off: r * 1024 + c * 128 + off + 64, :]
            ev = p.op("sync", lambda e, src=src, par=par, r=r: e.dma_start(
                out=KP[par][0:64, r * 1024:(r + 1) * 1024], in_=src),
                waits=[ev_cc, ev_kpm] + kp_free[par], inc=f"kp{par}", inc_n=16)
        return ev

    def load_vs(kvh):
        c, off = 4 + kvh // 2, 64 * (kvh % 2)
        ev = None
        for r in range(2):
            src = bounce_out.ap()[r * 1024 + c * 128 + off: r * 1024 + c * 128 + off + 64, :]
            ev = p.op("sync", lambda e, src=src, r=r: e.dma_start(
                out=VS[0:64, r * 1024:(r + 1) * 1024], in_=src),
                waits=[ev_cc] + vs_free, inc="vs", inc_n=16)
        return ev

    def load_qp(hh):
        buf = hh % 2
        chunk, p0 = hh // 2, 64 * (hh % 2)
        src = HE[p0:p0 + 64, chunk * T:(chunk + 1) * T]
        return p.op("sync", lambda e, src=src, buf=buf: e.dma_start(
            out=QP[buf][0:64, 0:T], in_=src),
            waits=[q_done, ev_kvb] + qp_free[buf], inc=f"qp{buf}", inc_n=16)

    kp_evs = {0: load_kp(0)}
    vs_evs = {0: load_vs(0)}
    qp_evs = {0: load_qp(0)}
    pt_free = [None] * 4
    o_done = None

    for kvh in range(NKV):
        par = kvh % 2
        ev_kpd = kp_evs[kvh]
        ev_vsd = vs_evs[kvh]
        evt = None
        for halfk in range(2):
            for k8 in range(8):
                kc = halfk * 8 + k8
                wl = ([ev_vsd, ev_idb] + BB.pw() + ([vt_free[par]] if (halfk == 0 and vt_free[par]) else [])) \
                    if k8 == 0 else ()
                evt = p.op("tensor", lambda e, kc=kc, k8=k8: e.transpose(
                    PSB.ap()[:, k8 * 64:(k8 + 1) * 64], VS[0:64, kc * 128:(kc + 1) * 128],
                    IDB[0:64, 0:64]),
                    waits=wl, inc="pe" if k8 == 7 else None)
            BB.wrote(evt)
            dst = VT[par].rearrange("p (k c) -> p k c", k=KC)[:, halfk * 8:(halfk + 1) * 8, 0:64]
            evv = p.op("scalar", lambda e, dst=dst: e.copy(
                dst, PSB.ap()[:, 0:512].rearrange("p (k c) -> p k c", c=64)),
                waits=BB.rw(), inc="act")
            BB.read(evv)
        if kvh + 1 < NKV:
            vs_free = [evt]
            vs_evs[kvh + 1] = load_vs(kvh + 1)
        ev_vt = evv
        ev_ksq = p.op("vector", lambda e, par=par: e.tensor_tensor(
            KSQ[0:64, :], KP[par][0:64, :], KP[par][0:64, :], ALU.mult),
            waits=[ev_kpd], inc="dve")
        ev_red = []
        for j in range(4):
            bk = B[6 if j % 2 == 0 else 3]
            evm = p.op("tensor", lambda e, j=j, bk=bk: e.matmul(
                bk.ap[64:65, 0:512], ONESB[0:64, 0:1], KSQ[0:64, j * 512:(j + 1) * 512],
                start=True, stop=True),
                waits=[ev_ksq] + bk.pw(), inc="pe")
            bk.wrote(evm)
            evr = p.op("vector", lambda e, j=j, bk=bk: e.reduce_max(
                AROW[64:65, j:j + 1], bk.ap[64:65, 0:512], axis=AX.X),
                waits=bk.rw(), inc="dve")
            bk.read(evr)
            ev_red.append(evr)
        p.op("vector", lambda e: e.tensor_tensor(
            AROW[64:65, 4:5], AROW[64:65, 0:1], AROW[64:65, 1:2], ALU.max), inc="dve")
        p.op("vector", lambda e: e.tensor_tensor(
            AROW[64:65, 5:6], AROW[64:65, 2:3], AROW[64:65, 3:4], ALU.max), inc="dve")
        ev_m2 = p.op("vector", lambda e: e.tensor_tensor(
            AROW[64:65, 4:5], AROW[64:65, 4:5], AROW[64:65, 5:6], ALU.max), inc="dve")

        def head_prologue(hh):
            buf = hh % 2
            ev_qpd = qp_evs[hh]
            ev_qsq = p.op("vector", lambda e, buf=buf: e.tensor_tensor(
                QSQ[0:64, :], QP[buf][0:64, :], QP[buf][0:64, :], ALU.mult),
                waits=[ev_qpd] + qsq_free, inc="dve")
            evq = None
            for th in range(2):
                bk = B[6]
                evm = p.op("tensor", lambda e, th=th, bk=bk: e.matmul(
                    bk.ap[64:65, 0:512], ONESB[0:64, 0:1], QSQ[0:64, th * THW:(th + 1) * THW],
                    start=True, stop=True),
                    waits=[ev_qsq] + bk.pw(), inc="pe")
                bk.wrote(evm)
                qsq_free[:] = [evm]
                evq = p.op("vector", lambda e, th=th, bk=bk: e.tensor_scalar(
                    SROW[64:65, th * THW:(th + 1) * THW], bk.ap[64:65, 0:512],
                    AROW[64:65, 4:5], 0.5, ALU.add, ALU.mult),
                    waits=bk.rw() + [ev_m2] + srow64_guard + b_read_guard, inc="dve")
                bk.read(evq)
            evb_ = p.op("scalar", lambda e, buf=buf: e.copy(
                QP[buf][64:65, 0:T], SROW[64:65, 0:T]), waits=[evq], inc="act")
            b_read_guard[:] = [evb_]
            return evb_

        ev_b_next = head_prologue(kvh * 4)
        for hq in range(4):
            hh = kvh * 4 + hq
            buf = hh % 2
            ev_b = ev_b_next
            if hh + 1 < NH:
                qp_evs[hh + 1] = load_qp(hh + 1)
            if hq < 3:
                ev_b_next = head_prologue(hh + 1)
            evsm = None
            for qt in range(2):
                av_evs = []
                hqt_i = (kvh * 4 + hq) * 2 + qt
                bO = B[3 + 2 * (hqt_i % 2)]
                sB = [B[0], B[1], B[2], B[4]]

                def s_mm(kc, first):
                    sbk = sB[kc % 4]
                    wl = ([ev_b, ev_kpd] if first else []) + sbk.pw()
                    ev = p.op("tensor", lambda e, kc=kc, par=par, buf=buf, qt=qt, sbk=sbk:
                              e.matmul(sbk.ap[:, :],
                                       KP[par][0:65, kc * 128:(kc + 1) * 128],
                                       QP[buf][0:65, qt * THW:(qt + 1) * THW],
                                       start=True, stop=True),
                              waits=wl, inc="pe")
                    sbk.wrote(ev)
                    return ev

                for kc in range(4):
                    evsm = s_mm(kc, kc == 0)
                for kc in range(KC):
                    sbk = sB[kc % 4]
                    pslot = kc % 4
                    wl2 = sbk.rw() + ([pt_free[pslot]] if pt_free[pslot] else [])
                    evex = p.op("scalar", lambda e, pslot=pslot, sbk=sbk: e.activation(
                        PT[pslot][:, :], sbk.ap[:, :], AF.Exp, bias=0.0, scale=0.125),
                        waits=wl2, inc="act")
                    sbk.read(evex)
                    wl3 = [evex, ev_vt] + (bO.pw() if kc == 0 else [])
                    evav = p.op("tensor", lambda e, kc=kc, par=par, pslot=pslot, bO=bO:
                                e.matmul(bO.ap[0:65, :],
                                         VT[par][:, kc * 65:(kc + 1) * 65], PT[pslot][:, :],
                                         start=(kc == 0), stop=(kc == KC - 1)),
                                waits=wl3, inc="pe")
                    av_evs.append(evav)
                    pt_free[pslot] = evav
                    if kc + 4 < KC:
                        evsm = s_mm(kc + 4, False)
                bO.wrote(av_evs[-1])
                evrc = p.op("vector", lambda e, bO=bO: e.reciprocal(
                    SROW[64:65, 0:512], bO.ap[64:65, 0:512]),
                    waits=bO.rw() + [ev_b_next], inc="dve")
                bO.read(evrc)
                bk6 = B[6]
                evbc = p.op("tensor", lambda e, bk6=bk6: e.matmul(
                    bk6.ap[0:64, 0:512], ONESR[64:65, 0:64], SROW[64:65, 0:512],
                    start=True, stop=True),
                    waits=[evrc] + bk6.pw(), inc="pe")
                bk6.wrote(evbc)
                srow64_guard[:] = [evbc]
                evrb = p.op("scalar", lambda e, bk6=bk6: e.copy(RB[0:64, :], bk6.ap[0:64, 0:512]),
                            waits=bk6.rw(), inc="act")
                bk6.read(evrb)
                chunk, p0 = hh // 2, 64 * (hh % 2)
                dst = XN[p0:p0 + 64, chunk * T + qt * THW: chunk * T + qt * THW + THW]
                o_done = p.op("vector", lambda e, dst=dst, bO=bO: e.tensor_tensor(
                    dst, bO.ap[0:64, 0:512], RB[0:64, :], ALU.mult),
                    waits=[evrb], inc="dve")
                bO.read(o_done)
            qp_free[buf] = [evsm]
            vt_free[par] = av_evs[-1]
        kp_free[par] = [evsm, ev_ksq]
        if kvh + 1 < NKV:
            kp_evs[kvh + 1] = load_kp(kvh + 1)

    if stop == "attn":
        return dump_and_finish([o_done], "RX")
    attn_res = None
    for dc in range(DC):
        sl = dc % 2
        ev_wd = dma_w("w1g", sl, wo_in[dc], cols=D)
        for th in range(2):
            gidx = dc * 2 + th
            bk = B[gidx % 4]
            evG = None
            for oc in range(DC):
                wl = ([ev_wd, o_done] + bk.pw()) if oc == 0 else ()
                evG = p.op("tensor", lambda e, sl=sl, oc=oc, th=th, bk=bk: e.matmul(
                    bk.ap[:, :], W1G[sl][:, oc * 128:(oc + 1) * 128],
                    XN[:, oc * T + th * THW: oc * T + th * THW + THW],
                    start=(oc == 0), stop=(oc == DC - 1)),
                    waits=wl, inc="pe" if oc == DC - 1 else None)
            bk.wrote(evG)
            if th == 1:
                w_free[("w1g", sl)] = evG
            rslice = RES[:, dc * T + th * THW: dc * T + th * THW + THW]
            attn_res = p.op("vector", lambda e, rslice=rslice, bk=bk: e.tensor_tensor(
                rslice, rslice, bk.ap[:, :], ALU.add), waits=bk.rw(), inc="dve")
            bk.read(attn_res)

    if stop == "wo":
        return dump_and_finish([attn_res], "RX")
    xn3, col3 = emit_ln([attn_res], need_cols=True)
    if stop == "ln3":
        return dump_and_finish([xn3, col3], "RX")
    ffn2_done = emit_ffn(0, xn3, col3)
    if stop == "ffn2":
        return dump_and_finish([ffn2_done], "RXW")

    out_free = [None, None]
    n_out = 0
    for t in range(TT):
        sl = t % 2
        eve = None
        for cg in range(4):
            bk = B[cg % 2]
            evt = None
            for c4 in range(4):
                ci = cg * 4 + c4
                wl = []
                if c4 == 0:
                    wl = ([ffn2_done] if (t == 0 and cg == 0) else []) + bk.pw() + \
                        ([out_free[sl]] if (cg == 0 and out_free[sl]) else [])
                evt = p.op("tensor", lambda e, ci=ci, c4=c4, t=t, bk=bk: e.transpose(
                    bk.ap[:, c4 * 128:(c4 + 1) * 128],
                    RES[:, ci * T + t * 128: ci * T + (t + 1) * 128], IDF[:, :]),
                    waits=wl, inc="pe" if c4 == 3 else None)
            bk.wrote(evt)
            eve = p.op("scalar", lambda e, bk=bk, sl=sl, cg=cg: e.copy(
                TMX[sl][:, cg * 512:(cg + 1) * 512], bk.ap[:, 0:512]),
                waits=bk.rw() + ([out_free[sl]] if out_free[sl] else []), inc="act")
            bk.read(eve)
        ev_st = p.op("sync", lambda e, t=t, sl=sl: e.dma_start(out=o_out[t], in_=TMX[sl][:, :]),
                     waits=[eve], inc=f"odma{sl}", inc_n=16)
        n_out += 16
        out_free[sl] = ev_st
    p.wait("sync", ("odma0", 16 * 4))
    p.wait("sync", ("odma1", 16 * 4))

    p.emit()
    return nc


def _lhsT_tiles(W):
    K, M = W.shape
    return np.ascontiguousarray(
        W.reshape(K // 128, 128, M // 128, 128).transpose(2, 1, 0, 3).reshape(M // 128, 128, K))


def kernel(L0_attn_x, L0_attn_Wq, L0_attn_Wk, L0_attn_Wv, L0_attn_Wo,
           L0_ffn_W1, L0_ffn_Wg, L0_ffn_W2, L0_ffn_Ws1, L0_ffn_Wsg, L0_ffn_Ws2,
           L0_ffn_gate_w, L0_ffn_gate_b, L0_ffn_biases):
    x = np.asarray(L0_attn_x, np.float32)
    W1 = np.asarray(L0_ffn_W1, np.float32)
    Wg = np.asarray(L0_ffn_Wg, np.float32)
    W2 = np.asarray(L0_ffn_W2, np.float32)

    w1g = np.empty((NX, DC, 128, 2 * D), np.float32)
    w2t = np.empty((NX, DC, 128, D), np.float32)
    for e in range(NE):
        w1g[e, :, :, 0:D] = _lhsT_tiles(W1[e])
        w1g[e, :, :, D:2 * D] = _lhsT_tiles(Wg[e])
        w2t[e] = _lhsT_tiles(W2[e])
    w1g[NE, :, :, 0:D] = _lhsT_tiles(np.asarray(L0_ffn_Ws1, np.float32))
    w1g[NE, :, :, D:2 * D] = _lhsT_tiles(np.asarray(L0_ffn_Wsg, np.float32))
    w2t[NE] = _lhsT_tiles(np.asarray(L0_ffn_Ws2, np.float32))
    wqt = _lhsT_tiles(np.asarray(L0_attn_Wq, np.float32))
    wkt = _lhsT_tiles(np.asarray(L0_attn_Wk, np.float32))
    wvt = _lhsT_tiles(np.asarray(L0_attn_Wv, np.float32))
    wot = _lhsT_tiles(np.asarray(L0_attn_Wo, np.float32))
    gw = np.asarray(L0_ffn_gate_w, np.float32)
    gwt = np.ascontiguousarray(gw.reshape(DC, 128, 7).transpose(1, 0, 2).reshape(128, DC * 7))
    brr = np.ascontiguousarray(np.asarray(L0_ffn_biases, np.float32).reshape(1, 7))
    gbr = np.ascontiguousarray(np.asarray(L0_ffn_gate_b, np.float32).reshape(1, 7))

    nc = build()
    shared = {"w1g": w1g, "w2t": w2t, "wqt": wqt, "wkt": wkt, "wvt": wvt, "wot": wot,
              "gwt": gwt, "br": brr, "gb": gbr}
    in_maps = []
    for c in range(8):
        b, h = divmod(c, 2)
        m = dict(shared)
        m["x"] = np.ascontiguousarray(x[b, h * T:(h + 1) * T, :].reshape(TT, 128, D))
        in_maps.append(m)
    res = run_bass_kernel_spmd(nc, in_maps, core_ids=list(range(8)))
    out = np.empty((4, 2048, D), np.float32)
    for c in range(8):
        b, h = divmod(c, 2)
        out[b, h * T:(h + 1) * T, :] = np.asarray(res.results[c]["o"]).reshape(T, D)
    return out
